# version 06
# baseline (speedup 1.0000x reference)
import numpy as np
import concourse.bass as bass
import concourse.mybir as mybir
from concourse.bass_utils import run_bass_kernel_spmd

F32 = mybir.dt.float32
BF16 = mybir.dt.bfloat16
AF = mybir.ActivationFunctionType
ALU = mybir.AluOpType

NCORES = 8
ARENA_COLS = 80 * 1024
B = 2
T = 4096
TL = 1024
HID = 2048
NB = HID // 128
TC = 512
LN_EPS = 1e-5
GROUPS = [[0, 1, 2, 3], [4, 5, 6, 7]]

LAUNCH_GROUPS = [["l0a", "l0f", "l1a", "l1f"]]



def _lhsT_pack(w):
    din, dout = w.shape
    kb, ob = din // 128, dout // 128
    a = w.reshape(kb, 128, ob, 128).transpose(2, 1, 0, 3).reshape(ob, 128, din)
    return np.ascontiguousarray(a)


C_IDENT = (0, 128)
C_ONES = (128, 256)
C_IOTA4 = (256, 260)
C_IOTA8 = (260, 268)
C_IOTA6 = (268, 274)
C_IOTA4M = (274, 278)
C_IOTA8M = (278, 286)
C_M70 = (286, 287)
C_TRILA = (287, 543)
C_W = 543


def _consts_arr():
    ident = np.eye(128, dtype=np.float32)
    ones = np.ones((128, 128), np.float32)
    iota4 = np.tile(np.arange(4, dtype=np.float32)[None, :], (128, 1))
    iota8 = np.tile(np.arange(8, dtype=np.float32)[None, :], (128, 1))
    iota6 = np.tile(np.arange(6, dtype=np.float32)[None, :], (128, 1))
    j = np.arange(128)[:, None]
    tt = np.arange(256)[None, :]
    trilA = (j <= tt).astype(np.float32)
    return np.ascontiguousarray(np.concatenate(
        [ident, ones, iota4, iota8, iota6, iota4 - 4.0, iota8 - 8.0,
         np.full((128, 1), -70.0, np.float32), trilA], axis=1))



class Sem:
    __slots__ = ("h", "n")

    def __init__(self, nc, name):
        self.h = nc.alloc_semaphore(name)
        self.n = 0

    def inc(self, instr, k=1):
        instr.then_inc(self.h, k)
        self.n += k
        return self.n

    def wait(self, eng, n=None):
        eng.wait_ge(self.h, self.n if n is None else n)


class KB:
    def __init__(self, nc):
        self.nc = nc
        self.t = nc.tensor
        self.v = nc.vector
        self.s = nc.scalar
        self.g = nc.gpsimd
        self.sy = nc.sync
        self.inputs = {}
        self.outputs = {}
        self.dram = {}
        self.sb = {}
        self.s_w = Sem(nc, "s_w")
        self.s_y = Sem(nc, "s_y")
        self.s_z = Sem(nc, "s_z")
        self.s_mm = Sem(nc, "s_mm")
        self.s_ve = Sem(nc, "s_ve")
        self.s_ac = Sem(nc, "s_ac")
        self.s_go = Sem(nc, "s_go")
        self.s_st = Sem(nc, "s_st")
        self.ncc = 0
        self._sems = {}
        self.AR = nc.alloc_sbuf_tensor("arena", [128, ARENA_COLS], BF16)
        self.ARF = self.AR[:, :].bitcast(F32)
        self.ps = [nc.alloc_psum_tensor(f"ps{i}", [128, 512], F32)
                   for i in range(8)]
        self.all_sems = [self.s_w, self.s_y, self.s_z, self.s_mm, self.s_ve,
                         self.s_ac, self.s_go]

    def sem(self, name):
        if name not in self._sems:
            self._sems[name] = Sem(self.nc, name)
        return self._sems[name]

    def engines(self):
        return [self.t, self.v, self.s, self.g, self.sy]

    @staticmethod
    def dma(eng, **kw):
        try:
            return eng.dma_start(**kw)
        except ValueError:
            return eng.dma_start(**kw)

    def vsync(self):
        self.v.wait_ge(self.s_ve.h, self.s_ve.n)

    def ssync(self):
        self.s.wait_ge(self.s_ac.h, self.s_ac.n)

    def barrier(self):
        for eng in self.engines():
            for sem in self.all_sems + list(self._sems.values()):
                if sem.n:
                    eng.wait_ge(sem.h, sem.n)

    def cc_sem(self):
        s = Sem(self.nc, f"s_cc{self.ncc}")
        self.ncc += 1
        return s

    def inp(self, name, shape, dtype=F32):
        if name not in self.inputs:
            self.inputs[name] = self.nc.dram_tensor(
                name, list(shape), dtype, kind="ExternalInput").ap()
        return self.inputs[name]

    def outp(self, name, shape, dtype=F32):
        if name not in self.outputs:
            self.outputs[name] = self.nc.dram_tensor(
                name, list(shape), dtype, kind="ExternalOutput").ap()
        return self.outputs[name]

    def scratch(self, name, shape, dtype):
        if name not in self.dram:
            self.dram[name] = self.nc.dram_tensor(name, list(shape), dtype).ap()
        return self.dram[name]

    def sbuf(self, name, shape, dtype):
        if name not in self.sb:
            self.sb[name] = self.nc.alloc_sbuf_tensor(name, list(shape), dtype)
        return self.sb[name]

    def load_consts(self, width=C_W):
        cst = self.inp("consts", [128, C_W], F32)
        self.cf = self.sbuf("consts_f", [128, width], F32)
        self.cbf = self.sbuf("consts_b", [128, width], BF16)
        self.s_w.inc(self.g.dma_start(out=self.cf[:, :],
                                      in_=cst[:, 0:width]), 16)
        self.s_w.inc(self.g.dma_start(out=self.cbf[:, :],
                                      in_=cst[:, 0:width]), 16)

    def cs(self, which, dtype=BF16):
        t = self.cbf if dtype == BF16 else self.cf
        a, b = which
        return t[:, a:b]

    def ones_col(self, dtype=BF16):
        return self.cs(C_ONES, dtype)[:, 0:1]

    def ones_row(self, dtype=BF16):
        return self.cs(C_ONES, dtype)[0:1, :]


def nc_switch4(kb):
    g = kb.g
    if not hasattr(kb, "_sw_idx"):
        pid = g.partition_id()
        kb._sw_idx = g.compute_val(pid % 4)
    return kb.nc.Switch(engines=[g], index=[kb._sw_idx], n=4)


def emit_load_xb(kb, x_src_ap):
    xb = kb.sbuf("xb", [128, NB * TL], BF16)
    for i in range(NB):
        kb.s_w.inc(kb.g.dma_start(out=xb[:, i * TL:(i + 1) * TL],
                                  in_=x_src_ap[i * 128:(i + 1) * 128, :]), 16)
    return xb



def emit_postln(kb, src_of, x_read_ap, x_write_aps, last=False, y_of=None,
                xin_buf=None, aux=None):
    t, v, s, sy = kb.t, kb.v, kb.s, kb.sy
    xb = kb.sb["xb"]
    assert y_of is not None and xin_buf is not None and aux is not None
    y2, st_, mub, rsb = aux["y2"], aux["stat"], aux["mub"], aux["rsb"]

    p_s, p_q = kb.ps[4], kb.ps[5]
    p_b0, p_b1 = kb.ps[6], kb.ps[7]

    sx = [kb.sem("ep_x0"), kb.sem("ep_x1")]
    sy.wait_ge(kb.s_y.h, kb.s_y.n)
    ve_xin = [0, 0]
    mm_y2 = [0, 0]
    wsem = kb.s_st if last else kb.s_y
    for tcx in range(2):
        tok = slice(tcx * TC, (tcx + 1) * TC)
        v.wait_ge(kb.s_ac.h, kb.s_ac.n)
        if wsem.n:
            v.wait_ge(wsem.h, wsem.n)
        for ob in range(NB):
            par = ob % 2
            xi = xin_buf[:, par * TC:(par + 1) * TC]
            if ve_xin[par]:
                sy.wait_ge(kb.s_ve.h, ve_xin[par])
            sx[par].inc(sy.dma_start(
                out=xi, in_=x_read_ap[ob * 128:(ob + 1) * 128, tok]), 16)
            src, waits = src_of(ob, tcx)
            sx[par].wait(v)
            for (sm, cnt) in waits:
                v.wait_ge(sm.h, cnt)
            yo = y_of(ob, tcx)
            kb.s_ve.inc(v.tensor_add(yo, src, xi))
            ve_xin[par] = kb.s_ve.n
            y_done = kb.s_ve.n
            y2i = y2[:, par * TC:(par + 1) * TC]
            s.wait_ge(kb.s_ve.h, y_done)
            if mm_y2[par]:
                s.wait_ge(kb.s_mm.h, mm_y2[par])
            kb.s_ac.inc(s.activation(y2i, yo, AF.Square))
            t.wait_ge(kb.s_ac.h, kb.s_ac.n)
            t.wait_ge(kb.s_ve.h, y_done)
            i1 = t.matmul(p_s[0:1, :], kb.ones_col(F32), yo,
                          start=(ob == 0), stop=(ob == NB - 1))
            i2 = t.matmul(p_q[0:1, :], kb.ones_col(BF16), y2i,
                          start=(ob == 0), stop=(ob == NB - 1))
            kb.s_mm.inc(i2)
            mm_y2[par] = kb.s_mm.n
            if ob == NB - 1:
                kb.s_mm.inc(i1)
        s0 = st_[0:1, 0:TC]
        s1 = st_[0:1, TC:2 * TC]
        s2 = st_[0:1, 2 * TC:3 * TC]
        mu, rstd, t2 = s0, s1, s2
        v.wait_ge(kb.s_mm.h, kb.s_mm.n)
        kb.s_ve.inc(v.tensor_copy(s0, p_s[0:1, :]))
        kb.s_ve.inc(v.tensor_copy(s1, p_q[0:1, :]))
        kb.vsync()
        kb.s_ve.inc(v.tensor_scalar(s0, s0, 1.0 / HID, None, ALU.mult))
        kb.vsync()
        kb.s_ve.inc(v.tensor_tensor(s2, s0, s0, ALU.mult))
        kb.vsync()
        kb.s_ve.inc(v.scalar_tensor_tensor(s1, s1, 1.0 / HID, s2,
                                           ALU.mult, ALU.subtract))
        kb.vsync()
        kb.s_ve.inc(v.tensor_scalar(s1, s1, LN_EPS, None, ALU.add))
        s.wait_ge(kb.s_ve.h, kb.s_ve.n)
        kb.s_ac.inc(s.activation(s1, s1, AF.Sqrt))
        v.wait_ge(kb.s_ac.h, kb.s_ac.n)
        kb.s_ve.inc(v.reciprocal(s1, s1))
        kb.vsync()
        kb.s_ve.inc(v.tensor_tensor(s2, s0, s1, ALU.mult))
        t.wait_ge(kb.s_ve.h, kb.s_ve.n)
        kb.s_mm.inc(t.matmul(p_b0[:, :], kb.ones_row(F32), rstd,
                             start=True, stop=True))
        kb.s_mm.inc(t.matmul(p_b1[:, :], kb.ones_row(F32), t2,
                             start=True, stop=True))
        v.wait_ge(kb.s_mm.h, kb.s_mm.n)
        kb.s_ve.inc(v.tensor_copy(rsb[:, :], p_b0[:, :]))
        kb.s_ve.inc(v.tensor_copy(mub[:, :], p_b1[:, :]))
        bc = kb.s_ve.n
        for ob in range(NB):
            yo = y_of(ob, tcx)
            v.wait_ge(kb.s_ve.h, bc)
            kb.s_ve.inc(v.tensor_tensor(yo, yo, rsb[:, :], ALU.mult))
            kb.vsync()
            kb.s_ve.inc(v.tensor_tensor(yo, yo, mub[:, :], ALU.subtract))
            nrm = kb.s_ve.n
            s.wait_ge(kb.s_ve.h, nrm)
            kb.s_ac.inc(s.activation(
                xb[:, ob * TL + tcx * TC: ob * TL + tcx * TC + TC],
                yo, AF.Copy))
            sy.wait_ge(kb.s_ve.h, nrm)
            for ap in x_write_aps:
                wsem.inc(sy.dma_start(out=ap[ob * 128:(ob + 1) * 128, tok],
                                      in_=yo), 16)



def emit_l0_ffn(kb, x_read, x_writes, last=False):
    t, v, s, g, sy = kb.t, kb.v, kb.s, kb.g, kb.sy
    xb = kb.sb["xb"]
    HB = 64
    wg = kb.inp("l0f_wg", [HB, 128, HID])
    wu = kb.inp("l0f_wu", [HB, 128, HID])
    wd = kb.inp("l0f_wd", [4, 128, 4 * 8192])
    hpark = kb.scratch("l0f_hpark", [8192, TL], BF16)

    AR, ARF = kb.AR, kb.ARF
    wg_sb = AR[:, 0:4096]
    wu_sb = AR[:, 4096:8192]
    h_sb = AR[:, 8192:10240]
    sg_sb = AR[:, 10240:12288]
    ep_y = ARF[:, 23552:31744]
    ep_xin = ARF[:, 31744:32768]
    ep_aux = {"y2": AR[:, 46080:47104], "stat": ARF[0:1, 32768:34304],
              "mub": ARF[:, 34304:34816], "rsb": ARF[:, 34816:35328]}

    kb.barrier()

    swp = [kb.sem("l0f_w0"), kb.sem("l0f_w1")]
    shw = [kb.sem("l0f_hw0"), kb.sem("l0f_hw1")]
    mm_at = [0, 0]
    ac_at = [0, 0]
    ve_at = [0, 0]
    dma_at = [0, 0]
    for hb in range(HB):
        par = hb % 2
        wgs = wg_sb[:, par * HID:(par + 1) * HID]
        wus = wu_sb[:, par * HID:(par + 1) * HID]
        if mm_at[par]:
            g.wait_ge(kb.s_mm.h, mm_at[par])
        swp[par].inc(g.dma_start(out=wgs, in_=wg[hb]), 16)
        swp[par].inc(g.dma_start(out=wus, in_=wu[hb]), 16)
        pg = (kb.ps[4 * par + 0], kb.ps[4 * par + 1])
        pu = (kb.ps[4 * par + 2], kb.ps[4 * par + 3])
        swp[par].wait(t)
        if ac_at[par]:
            t.wait_ge(kb.s_ac.h, ac_at[par])
            t.wait_ge(kb.s_ve.h, ve_at[par])
        for half in range(2):
            for k in range(NB):
                rhs = xb[:, k * TL + half * 512: k * TL + half * 512 + 512]
                st, sp = (k == 0), (k == NB - 1)
                i1 = t.matmul(pg[half][:, :],
                              wgs[:, k * 128:(k + 1) * 128], rhs,
                              start=st, stop=sp)
                i2 = t.matmul(pu[half][:, :],
                              wus[:, k * 128:(k + 1) * 128], rhs,
                              start=st, stop=sp)
                if sp:
                    kb.s_mm.inc(i1)
                    kb.s_mm.inc(i2)
        mm_at[par] = kb.s_mm.n
        hs = h_sb[:, par * TL:(par + 1) * TL]
        sgs = sg_sb[:, par * TL:(par + 1) * TL]
        s.wait_ge(kb.s_mm.h, mm_at[par])
        for half in range(2):
            kb.s_ac.inc(s.activation(sgs[:, half * 512:half * 512 + 512],
                                     pg[half][:, :], AF.Silu))
        ac_at[par] = kb.s_ac.n
        v.wait_ge(kb.s_ac.h, ac_at[par])
        if dma_at[par]:
            v.wait_ge(shw[par].h, dma_at[par])
        for half in range(2):
            kb.s_ve.inc(v.tensor_tensor(hs[:, half * 512:half * 512 + 512],
                                        pu[half][:, :],
                                        sgs[:, half * 512:half * 512 + 512],
                                        ALU.mult))
        ve_at[par] = kb.s_ve.n
        sy.wait_ge(kb.s_ve.h, ve_at[par])
        shw[par].inc(sy.dma_start(out=hpark[hb * 128:(hb + 1) * 128, :],
                                  in_=hs), 16)
        dma_at[par] = shw[par].n

    t.wait_ge(kb.s_ac.h, kb.s_ac.n)
    t.wait_ge(kb.s_ve.h, kb.s_ve.n)

    wd_sb = AR[:, 12288:45056]
    hh_sb = AR[:, 45056:46080]
    swd = kb.sem("l0f_wd")
    shh = [kb.sem("l0f_h0"), kb.sem("l0f_h1")]
    state = {"mm_down": 0, "mm_chunk": [0, 0]}

    def src_of(ob, tcx):
        og, j = ob // 4, ob % 4
        if j == 0:
            g.wait_ge(kb.s_ve.h, kb.s_ve.n)
            swd.inc(g.dma_start(out=wd_sb[:, :], in_=wd[og]), 16)
            swd.wait(t)
            t.wait_ge(kb.s_ve.h, kb.s_ve.n)
            sy.wait_ge(shw[0].h, shw[0].n)
            sy.wait_ge(shw[1].h, shw[1].n)
            for k in range(HB):
                par = k % 2
                hh = hh_sb[:, par * 512:(par + 1) * 512]
                if state["mm_chunk"][par]:
                    sy.wait_ge(kb.s_mm.h, state["mm_chunk"][par])
                shh[par].inc(sy.dma_start(
                    out=hh,
                    in_=hpark[k * 128:(k + 1) * 128,
                              tcx * TC:(tcx + 1) * TC]), 16)
                shh[par].wait(t)
                for jj in range(4):
                    ii = t.matmul(kb.ps[jj][:, :],
                                  wd_sb[:, jj * 8192 + k * 128:
                                        jj * 8192 + k * 128 + 128],
                                  hh, start=(k == 0), stop=(k == HB - 1))
                kb.s_mm.inc(ii)
                state["mm_chunk"][par] = kb.s_mm.n
            state["mm_down"] = kb.s_mm.n
        return kb.ps[j][:, :], [(kb.s_mm, state["mm_down"])]

    emit_postln(kb, src_of, x_read, x_writes, last=last,
                y_of=lambda ob, tcx: ep_y[:, ob * TC:(ob + 1) * TC],
                xin_buf=ep_xin, aux=ep_aux)


def emit_l0_attn(kb, x_read, x_writes, last=False):
    t, v, s, g, sy = kb.t, kb.v, kb.s, kb.g, kb.sy
    xb = kb.sb["xb"]
    H2, R = 2, 6
    wq = kb.inp("l0a_wq", [NB, 128, HID])
    wk = kb.inp("l0a_wk", [NB, 128, HID])
    wgp = kb.inp("l0a_wg", [32, 128, HID])
    wvp = kb.inp("l0a_wv", [8, 128, NB * 512])
    wop = kb.inp("l0a_wo", [NB, 128, 4096])
    qexp = kb.inp("l0a_qexp", [128, H2 * R * 2 * 256])
    kexp = kb.inp("l0a_kexp", [128, H2 * R * 2 * 256])
    wgate = kb.inp("l0a_wgate", [128, 8])

    qk_in = kb.scratch("l0a_qkstage", [2 * HID, TL], BF16)
    qk_out = kb.scratch("l0a_qkgat", [8 * HID, TL], BF16)
    v_in = kb.scratch("l0a_vstage", [TL, 4096], BF16)
    v_out = kb.scratch("l0a_vgat", [T, 4096], BF16)
    o_in = kb.scratch("l0a_ostage", [TL, T], BF16)
    o_oH = [kb.scratch(f"l0a_og{i}", [2048, T], BF16) for i in range(2)]
    oacc = kb.scratch("l0a_oacc", [TL, T], F32)
    gpark = kb.scratch("l0a_gpark", [4096, TL], BF16)

    AR, ARF = kb.AR, kb.ARF
    v_sb = AR[:, 0:16384]
    ke_sb = AR[:, 16384:24576]
    mb_sb = AR[:, 24576:28672]
    qxp_sb = AR[:, 28672:31744]
    kxp_sb = AR[:, 31744:34816]
    rwr = AR[0:1, 34816:38912]
    qe_sb = AR[:, 38912:39936]
    wnb = AR[:, 39936:40448]
    wn_row = AR[0:1, 40448:40960]
    q_sb = AR[:, 40960:49152]
    k_sb = AR[:, 49152:57344]
    rstg = AR[0:6, 57344:57472]
    wst_a = AR[:, 0:4096]
    wvs_a = AR[:, 4096:20480]
    st_b = AR[:, 20480:24576]
    o_loc = AR[:, 0:32768]
    sg_f = AR[:, 32768:34816]
    wo_f = AR[:, 34816:43008]
    g_f = AR[:, 43008:45056]
    y2_l0a = AR[:, 45056:46080]
    stf = ARF[:, 28736:32832]
    Ssh = ARF[:, 32832:33856]
    dn_row = ARF[0:1, 33856:34368]
    rw_sb = ARF[:, 34368:34560]
    sm_sb = ARF[:, 34560:34752]
    lg_sb = ARF[:, 34752:34768]
    t4 = ARF[:, 34768:34800]
    ep_y_l0a = ARF[:, 28736:36928]
    ep_xin_l0a = ARF[:, 36928:37952]
    mub_l0a = ARF[:, 37952:38464]
    rsb_l0a = ARF[:, 38464:38976]
    stat_l0a = ARF[0:1, 38976:40512]
    rwsd = kb.scratch("l0a_rwsd", [6, 4096], BF16)

    P_sb = xb[:, :]

    kb.barrier()
    swp = [kb.sem("l0a_w0"), kb.sem("l0a_w1")]
    sst = [kb.sem("l0a_s0"), kb.sem("l0a_s1")]
    sqt = [kb.sem("l0a_q0"), kb.sem("l0a_q1")]
    skt = [kb.sem("l0a_k0"), kb.sem("l0a_k1")]
    svr = kb.sem("l0a_vrd")
    sxp = kb.sem("l0a_xp")
    soa = [kb.sem("l0a_oa0"), kb.sem("l0a_oa1")]
    sob = kb.sem("l0a_ob")
    sgf = [kb.sem("l0a_g0"), kb.sem("l0a_g1")]
    swo = [kb.sem("l0a_wo0"), kb.sem("l0a_wo1")]
    sol = kb.sem("l0a_ol")

    pid4 = None

    mm_at = [0, 0]
    ve_at = [0, 0]
    st_at = [0, 0]

    def proj_fm(wpack, n_ob, dst, dst_row0):
        for ob in range(n_ob):
            par = ob % 2
            ws = wst_a[:, par * HID:(par + 1) * HID]
            if mm_at[par]:
                g.wait_ge(kb.s_mm.h, mm_at[par])
            swp[par].inc(g.dma_start(out=ws, in_=wpack[ob]), 16)
            swp[par].wait(t)
            pq = (kb.ps[4 * par + 0], kb.ps[4 * par + 1])
            if ve_at[par]:
                t.wait_ge(kb.s_ve.h, ve_at[par])
            for half in range(2):
                for k in range(NB):
                    ii = t.matmul(pq[half][:, :],
                                  ws[:, k * 128:(k + 1) * 128],
                                  xb[:, k * TL + half * 512:
                                     k * TL + half * 512 + 512],
                                  start=(k == 0), stop=(k == NB - 1))
                kb.s_mm.inc(ii)
            mm_at[par] = kb.s_mm.n
            stg = st_b[:, par * TL:(par + 1) * TL]
            v.wait_ge(kb.s_mm.h, mm_at[par])
            if st_at[par]:
                v.wait_ge(sst[par].h, st_at[par])
            for half in range(2):
                kb.s_ve.inc(v.tensor_copy(stg[:, half * 512:half * 512 + 512],
                                          pq[half][:, :]))
            ve_at[par] = kb.s_ve.n
            sy.wait_ge(kb.s_ve.h, ve_at[par])
            sst[par].inc(sy.dma_start(
                out=dst[dst_row0 + ob * 128: dst_row0 + (ob + 1) * 128, :],
                in_=stg), 16)
            st_at[par] = sst[par].n

    proj_fm(wq, NB, qk_in, 0)
    proj_fm(wk, NB, qk_in, HID)

    g.wait_ge(sst[0].h, sst[0].n)
    g.wait_ge(sst[1].h, sst[1].n)
    cc1 = kb.cc_sem()
    cc1.inc(g.collective_compute(
        "AllGather", ALU.bypass,
        replica_groups=GROUPS,
        ins=[qk_in[:, :]],
        outs=[qk_out[:, :]],
    ), 1)

    t.wait_ge(kb.s_ve.h, kb.s_ve.n)
    vw_at = list(mm_at)
    vv_at = [0, 0]
    for ch in range(8):
        par = ch % 2
        wvs = wvs_a[:, par * NB * 512:(par + 1) * NB * 512]
        if vw_at[par]:
            g.wait_ge(kb.s_mm.h, vw_at[par])
        swp[par].inc(g.dma_start(out=wvs, in_=wvp[ch]), 16)
        swp[par].wait(t)
        if ch > 0:
            t.wait_ge(kb.s_ve.h, vv_at[1 - par])
        for k in range(NB):
            for tb in range(8):
                ii = t.matmul(kb.ps[tb][:, :],
                              xb[:, k * TL + tb * 128:
                                 k * TL + tb * 128 + 128],
                              wvs[:, k * 512:(k + 1) * 512],
                              start=(k == 0), stop=(k == NB - 1))
            kb.s_mm.inc(ii)
        vw_at[par] = kb.s_mm.n
        v.wait_ge(kb.s_mm.h, vw_at[par])
        v.wait_ge(sst[0].h, sst[0].n)
        v.wait_ge(sst[1].h, sst[1].n)
        for tb in range(8):
            kb.s_ve.inc(v.tensor_copy(st_b[:, tb * 512:(tb + 1) * 512],
                                      kb.ps[tb][:, :]))
        vv_at[par] = kb.s_ve.n
        sy.wait_ge(kb.s_ve.h, vv_at[par])
        sst[0].inc(sy.dma_start(
            out=v_in.rearrange("(tb p) c -> p tb c", p=128)[
                :, :, ch * 512:(ch + 1) * 512],
            in_=st_b.rearrange("p (tb c) -> p tb c", tb=8)), 16)

    g.wait_ge(sst[0].h, sst[0].n)
    cc1b = kb.cc_sem()
    cc1b.inc(g.collective_compute(
        "AllGather", ALU.bypass,
        replica_groups=GROUPS,
        ins=[v_in[:, :]],
        outs=[v_out[:, :]],
    ), 1)

    mm_at = list(vw_at)
    mv = max(vv_at)
    ve_at = [mv, mv]
    v.wait_ge(sst[0].h, sst[0].n)
    st_at = [0, 0]
    proj_fm(wgp, 32, gpark, 0)
    mm_gp = kb.s_mm.n
    sst_gp = (sst[0].n, sst[1].n)

    wga_sb = kb.sbuf("l0a_wga", [128, 8], BF16)
    g.wait_ge(kb.s_mm.h, kb.s_mm.n)
    sxp.inc(g.dma_start(out=wga_sb[:, :], in_=wgate), 16)
    svv = kb.sem("l0a_vvr")

    def qxp_tile(src, r, eb, db):
        base = (r * 2 + eb) * 256 + db * 128
        return src[:, base:base + 128]

    onesb = kb.ones_col(BF16)

    cc_oh = []
    for h in range(H2):
        if h == 0:
            g.wait_ge(cc1.h, 1)
        if h > 0:
            g.wait_ge(kb.s_mm.h, kb.s_mm.n)
        sxp.inc(g.dma_start(out=qxp_sb,
                            in_=qexp[:, h * 3072:(h + 1) * 3072]), 16)
        sxp.inc(g.dma_start(out=kxp_sb,
                            in_=kexp[:, h * 3072:(h + 1) * 3072]), 16)
        qk_r = qk_out.rearrange("(r4 rows) t -> rows r4 t", rows=4096)
        for i4 in nc_switch4(kb):
            for db in range(2):
                qrow_q = i4 * 512 + h * 256 + db * 128
                g.dma_start(
                    out=q_sb.rearrange("p (db r4 t) -> p db r4 t",
                                       db=2, r4=4)[:, db],
                    in_=qk_r[qrow_q:qrow_q + 128]).then_inc(svr.h, 16)
                g.dma_start(
                    out=k_sb.rearrange("p (db r4 t) -> p db r4 t",
                                       db=2, r4=4)[:, db],
                    in_=qk_r[2048 + qrow_q:2048 + qrow_q + 128]).then_inc(
                        svr.h, 16)
        svr.n += 4 * 16

        sxp.wait(t)
        svr.wait(t)
        t.wait_ge(kb.s_ve.h, kb.s_ve.n)
        t.wait_ge(kb.s_ac.h, kb.s_ac.n)
        if h == 0:
            v.wait_ge(sst[0].h, sst_gp[0])
            v.wait_ge(sst[1].h, sst_gp[1])
        lg_mm = []
        for tile in range(32):
            par = tile % 2
            pb = kb.ps[4 + par]
            if tile >= 2:
                t.wait_ge(kb.s_ac.h, kb.s_ac.n)
            for db in range(2):
                ii = t.matmul(pb[:, 0:4],
                              q_sb[:, db * 4096 + tile * 128:
                                   db * 4096 + tile * 128 + 128],
                              wga_sb[:, db * 4:(db + 1) * 4],
                              start=(db == 0), stop=(db == 1))
            kb.s_mm.inc(ii)
            lg_mm.append(kb.s_mm.n)
            s.wait_ge(kb.s_mm.h, kb.s_mm.n)
            m4 = t4[:, 0:1]
            sm4 = t4[:, 1:2]
            i1 = t4[:, 2:3]
            m1 = t4[:, 3:4]
            m2 = t4[:, 4:5]
            wsumr = t4[:, 5:6]
            i1p = t4[:, 6:7]
            sc4 = lg_sb[:, 0:4]
            eq4 = t4[:, 8:12]
            tt4 = t4[:, 12:16]
            oh6a = t4[:, 16:22]
            oh6b = t4[:, 22:28]
            v.wait_ge(kb.s_mm.h, kb.s_mm.n)
            kb.s_ve.inc(v.reduce_max(m4, pb[:, 0:4], mybir.AxisListType.X,
                                     negate=True))
            kb.vsync()
            s.wait_ge(kb.s_ve.h, kb.s_ve.n)
            kb.s_ac.inc(s.activation(sc4, pb[:, 0:4], AF.Exp, bias=m4,
                                     accum_out=sm4))
            v.wait_ge(kb.s_ac.h, kb.s_ac.n)
            kb.s_ve.inc(v.reciprocal(sm4, sm4))
            kb.vsync()
            kb.s_ve.inc(v.tensor_scalar(sc4, sc4, sm4, None, ALU.mult))
            kb.vsync()
            iom = kb.cs(C_IOTA4M, F32)
            kb.s_ve.inc(v.reduce_max(m1, sc4, mybir.AxisListType.X))
            kb.vsync()
            kb.s_ve.inc(v.tensor_single_scalar(eq4, sc4, m1, ALU.is_equal))
            kb.vsync()
            kb.s_ve.inc(v.tensor_mul(tt4, eq4, iom))
            kb.vsync()
            kb.s_ve.inc(v.tensor_reduce(i1, tt4, mybir.AxisListType.X,
                                        ALU.min))
            kb.vsync()
            kb.s_ve.inc(v.tensor_single_scalar(eq4, iom, i1, ALU.is_equal))
            kb.vsync()
            kb.s_ve.inc(v.scalar_tensor_tensor(tt4, eq4, -1e9, sc4,
                                               ALU.mult, ALU.add))
            kb.vsync()
            kb.s_ve.inc(v.reduce_max(m2, tt4, mybir.AxisListType.X))
            kb.vsync()
            kb.s_ve.inc(v.tensor_scalar(i1p, i1, 6.0, None, ALU.add))
            kb.vsync()
            io6 = kb.cs(C_IOTA6, F32)
            kb.s_ve.inc(v.tensor_single_scalar(oh6a, io6, i1p, ALU.is_equal))
            kb.s_ve.inc(v.tensor_single_scalar(eq4, tt4, m2, ALU.is_equal))
            kb.vsync()
            kb.s_ve.inc(v.tensor_mul(tt4, eq4, iom))
            kb.vsync()
            kb.s_ve.inc(v.tensor_reduce(i1, tt4, mybir.AxisListType.X,
                                        ALU.min))
            kb.vsync()
            kb.s_ve.inc(v.tensor_scalar(i1p, i1, 6.0, None, ALU.add))
            kb.vsync()
            kb.s_ve.inc(v.tensor_single_scalar(oh6b, io6, i1p, ALU.is_equal))
            kb.vsync()
            kb.s_ve.inc(v.tensor_add(wsumr, m1, m2))
            kb.vsync()
            kb.s_ve.inc(v.reciprocal(wsumr, wsumr))
            kb.vsync()
            kb.s_ve.inc(v.tensor_scalar(m1, m1, wsumr, 0.5, ALU.mult,
                                        ALU.mult))
            kb.s_ve.inc(v.tensor_scalar(m2, m2, wsumr, 0.5, ALU.mult,
                                        ALU.mult))
            kb.vsync()
            rwt = rw_sb[:, tile * 6:(tile + 1) * 6]
            kb.s_ve.inc(v.memset(rwt[:, 0:2], 0.25))
            kb.s_ve.inc(v.tensor_scalar(oh6a, oh6a, m1, None, ALU.mult))
            kb.s_ve.inc(v.tensor_scalar(oh6b, oh6b, m2, None, ALU.mult))
            kb.vsync()
            kb.s_ve.inc(v.tensor_add(rwt[:, 2:6], oh6a[:, 2:6],
                                     oh6b[:, 2:6]))
            kb.vsync()
        kb.s_ve.inc(v.tensor_scalar(sm_sb[:, :], rw_sb[:, :], 0.0, None,
                                    ALU.is_gt))
        idf = kb.cs(C_IDENT, F32)
        srws = kb.sem("l0a_rws")
        for tile in range(32):
            t.wait_ge(kb.s_ve.h, kb.s_ve.n)
            kb.s_mm.inc(t.transpose(kb.ps[6][0:6, 0:128],
                                    rw_sb[:, tile * 6:(tile + 1) * 6], idf))
            v.wait_ge(kb.s_mm.h, kb.s_mm.n)
            if srws.n:
                v.wait_ge(srws.h, srws.n)
            kb.s_ve.inc(v.tensor_copy(rstg[:, :], kb.ps[6][0:6, 0:128]))
            t.wait_ge(kb.s_ve.h, kb.s_ve.n)
            sy.wait_ge(kb.s_ve.h, kb.s_ve.n)
            srws.inc(sy.dma_start(
                out=rwsd[:, tile * 128:(tile + 1) * 128], in_=rstg[:, :]), 16)
        if h == 0:
            g.wait_ge(cc1b.h, 1)
            g.wait_ge(kb.s_mm.h, mm_gp)
        for i4 in nc_switch4(kb):
            colh = i4 * 1024 + h * 512
            g.dma_start(
                out=v_sb.rearrange("p (tb c) -> p tb c", c=512),
                in_=v_out.rearrange("(tb p) c -> p tb c", p=128)[
                    :, :, colh:colh + 512]).then_inc(svv.h, 16)
        svv.n += 16
        g.wait_ge(srws.h, srws.n)

        ve_keb = [0, 0]
        ve_msk = 0
        mm_qe = 0
        ve_wnb = 0
        soa_t = [0, 0]
        for r in range(R):
            g.wait_ge(kb.s_mm.h, kb.s_mm.n)
            g.wait_ge(kb.s_ve.h, kb.s_ve.n)
            sob.inc(g.dma_start(out=rwr, in_=rwsd[r:r + 1, :]), 16)
            sob.wait(t)
            mm_mb = []
            for chn in range(8):
                w6 = max(ve_msk, ve_qe_evac if mm_qe else 0)
                if w6:
                    t.wait_ge(kb.s_ve.h, w6)
                i = t.matmul(
                    kb.ps[6][:, :], kb.ones_row(BF16),
                    rwr[:, chn * 512:(chn + 1) * 512],
                    start=True, stop=True)
                kb.s_mm.inc(i)
                mm_mb.append(kb.s_mm.n)
                v.wait_ge(kb.s_mm.h, kb.s_mm.n)
                kb.s_ve.inc(v.tensor_scalar(
                    mb_sb[:, chn * 512:(chn + 1) * 512], kb.ps[6][:, :],
                    0.0, None, ALU.is_gt))
                ve_msk = kb.s_ve.n
            for chn in range(8):
                par = chn % 2
                for eb in range(2):
                    pp = kb.ps[4 + par]
                    if ve_keb[par]:
                        t.wait_ge(kb.s_ve.h, ve_keb[par])
                    for db in range(2):
                        ii = t.matmul(pp[:, :],
                                      qxp_tile(kxp_sb, r, eb, db),
                                      k_sb[:, db * 4096 + chn * 512:
                                           db * 4096 + chn * 512 + 512],
                                      start=(db == 0), stop=(db == 1))
                    kb.s_mm.inc(ii)
                    v.wait_ge(kb.s_mm.h, kb.s_mm.n)
                    kb.s_ve.inc(v.tensor_tensor(
                        ke_sb[:, eb * 4096 + chn * 512:
                              eb * 4096 + chn * 512 + 512],
                        pp[:, :], mb_sb[:, chn * 512:(chn + 1) * 512],
                        ALU.mult))
                    ve_keb[par] = kb.s_ve.n
            ke_done = kb.s_ve.n
            if r == 0:
                svv.wait(t)
            for st_i in range(8):
                w6 = max(ve_msk, ve_qe_evac if mm_qe else 0, ve_wnb)
                if w6:
                    t.wait_ge(kb.s_ve.h, w6)
                for eb in range(2):
                    pq = kb.ps[6 + eb]
                    for db in range(2):
                        ii = t.matmul(pq[:, :],
                                      qxp_tile(qxp_sb, r, eb, db),
                                      q_sb[:, db * 4096 + st_i * 512:
                                           db * 4096 + st_i * 512 + 512],
                                      start=(db == 0), stop=(db == 1))
                    kb.s_mm.inc(ii)
                    mm_qe = kb.s_mm.n
                    v.wait_ge(kb.s_mm.h, mm_qe)
                    kb.s_ve.inc(v.tensor_scalar(
                        qe_sb[:, eb * 512:(eb + 1) * 512], pq[:, :],
                        0.0625, None, ALU.mult))
                    ve_qe_evac = kb.s_ve.n
                t.wait_ge(kb.s_ve.h, ve_qe_evac)
                ac_P = [0, 0]
                ac_of = {}
                ve_P2 = {}
                mm_S = [0, 0]
                for i in range(35):
                    if i < 32:
                        tk = i
                        par = tk % 2
                        pS = kb.ps[4 + par]
                        if ac_P[par]:
                            t.wait_ge(kb.s_ac.h, ac_P[par])
                        if tk == 0 and ke_done:
                            t.wait_ge(kb.s_ve.h, ke_done)
                        for eb in range(2):
                            ii = t.matmul(pS[:, :],
                                          ke_sb[:, eb * 4096 + tk * 128:
                                                eb * 4096 + tk * 128 + 128],
                                          qe_sb[:, eb * 512:(eb + 1) * 512],
                                          start=(eb == 0), stop=(eb == 1))
                        kb.s_mm.inc(ii)
                        mm_S[par] = kb.s_mm.n
                        s.wait_ge(kb.s_mm.h, mm_S[par])
                        kb.s_ac.inc(s.activation(
                            P_sb[:, tk * 512:(tk + 1) * 512], pS[:, :],
                            AF.Exp, bias=kb.cs(C_M70, F32)))
                        ac_P[par] = kb.s_ac.n
                        ac_of[tk] = kb.s_ac.n
                    td = i - 2
                    if 0 <= td < 32:
                        t.wait_ge(kb.s_ac.h, ac_of[td])
                        ii = t.matmul(kb.ps[7][0:1, :], onesb,
                                      P_sb[:, td * 512:(td + 1) * 512],
                                      start=(td == 0), stop=(td == 31))
                        kb.s_mm.inc(ii)
                        mm_dnl = kb.s_mm.n
                        if td == 31:
                            mm_dn = kb.s_mm.n
                        v.wait_ge(kb.s_mm.h, mm_dnl)
                        kb.s_ve.inc(v.tensor_scalar(
                            P_sb[:, td * 512:(td + 1) * 512],
                            P_sb[:, td * 512:(td + 1) * 512],
                            sm_sb[:, td * 6 + r: td * 6 + r + 1],
                            None, ALU.mult))
                        ve_P2[td] = kb.s_ve.n
                    ta = i - 3
                    if 0 <= ta < 32:
                        t.wait_ge(kb.s_ve.h, ve_P2[ta])
                        if ta == 0 and (soa_t[0] or soa_t[1]):
                            t.wait_ge(kb.s_ve.h, max(soa_t))
                        for dv in range(4):
                            ii = t.matmul(kb.ps[dv][:, :],
                                          v_sb[:, ta * 512 + dv * 128:
                                               ta * 512 + dv * 128 + 128],
                                          P_sb[:, ta * 512:(ta + 1) * 512],
                                          start=(ta == 0), stop=(ta == 31))
                        if ta == 31:
                            kb.s_mm.inc(ii)
                            mm_dn_av = kb.s_mm.n
                v.wait_ge(kb.s_mm.h, mm_dn)
                kb.s_ve.inc(v.reciprocal(dn_row, kb.ps[7][0:1, :]))
                kb.vsync()
                kb.s_ve.inc(v.tensor_tensor(
                    wn_row, rwr[:, st_i * 512:(st_i + 1) * 512], dn_row,
                    ALU.mult))
                t.wait_ge(kb.s_ve.h, kb.s_ve.n)
                kb.s_mm.inc(t.matmul(kb.ps[7][:, :], kb.ones_row(BF16),
                                     wn_row, start=True, stop=True))
                v.wait_ge(kb.s_mm.h, kb.s_mm.n)
                kb.s_ve.inc(v.tensor_copy(wnb, kb.ps[7][:, :]))
                ve_wnb = kb.s_ve.n
                kb.vsync()
                spar = st_i % 2
                stfp = stf[:, spar * 2048:(spar + 1) * 2048]
                v.wait_ge(kb.s_mm.h, mm_dn_av)
                if soa[spar].n:
                    v.wait_ge(soa[spar].h, soa[spar].n)
                for dv in range(4):
                    kb.s_ve.inc(v.tensor_tensor(
                        stfp[:, dv * 512:(dv + 1) * 512], kb.ps[dv][:, :],
                        wnb, ALU.mult))
                soa_t[spar] = kb.s_ve.n
                oev = kb.s_ve.n
                g.wait_ge(kb.s_ve.h, oev)
                if soa[spar].n:
                    g.wait_ge(soa[spar].h, soa[spar].n)
                soa[spar].inc(g.dma_start(
                    out=oacc.rearrange("(hh d p) c -> hh p d c",
                                       d=4, p=128)[h][
                        :, :, st_i * 512:(st_i + 1) * 512],
                    in_=stfp.rearrange("p (d c) -> p d c", d=4),
                    accum_op=(ALU.add if r > 0 else ALU.bypass)), 16)
        g.wait_ge(soa[0].h, soa[0].n)
        g.wait_ge(soa[1].h, soa[1].n)
        sob.inc(g.dma_start(out=o_in[h * 512:(h + 1) * 512, :],
                            in_=oacc[h * 512:(h + 1) * 512, :]), 16)
        g.wait_ge(sob.h, sob.n)
        cco = kb.cc_sem()
        cco.inc(g.collective_compute(
            "AllGather", ALU.bypass, replica_groups=GROUPS,
            ins=[o_in[h * 512:(h + 1) * 512, :]],
            outs=[o_oH[h][:, :]]), 1)
        cc_oh.append(cco)

    for cco in cc_oh:
        g.wait_ge(cco.h, 1)
    kb.s_go.inc(g.engine_nop())
    sy.wait_ge(kb.s_go.h, kb.s_go.n)
    sy.wait_ge(kb.s_mm.h, kb.s_mm.n)
    sy.wait_ge(kb.s_ve.h, kb.s_ve.n)
    for i4 in nc_switch4(kb):
        tok0 = i4 * TL
        for hh in range(2):
            g.dma_start(
                out=o_loc.rearrange("p (h2 rj t) -> p h2 rj t",
                                    h2=2, rj=16)[:, hh],
                in_=o_oH[hh].rearrange("(rj p) t -> p rj t", p=128)[
                    :, :, tok0:tok0 + TL]).then_inc(sol.h, 16)
    sol.n += 2 * 16
    gate_ve = []
    for pb in range(32):
        par = pb % 2
        gf = g_f[:, par * TL:(par + 1) * TL]
        if pb >= 2:
            sy.wait_ge(kb.s_ac.h, kb.s_ac.n)
        sgf[par].inc(sy.dma_start(
            out=gf, in_=gpark[pb * 128:(pb + 1) * 128, :]), 16)
        sgf[par].wait(s)
        sgg = sg_f[:, par * TL:(par + 1) * TL]
        if pb >= 2:
            s.wait_ge(kb.s_ve.h, gate_ve[pb - 2])
        kb.s_ac.inc(s.activation(sgg, gf, AF.Silu))
        v.wait_ge(kb.s_ac.h, kb.s_ac.n)
        if pb == 0:
            v.wait_ge(sol.h, sol.n)
        kb.s_ve.inc(v.tensor_tensor(o_loc[:, pb * TL:(pb + 1) * TL],
                                    o_loc[:, pb * TL:(pb + 1) * TL],
                                    sgg, ALU.mult))
        gate_ve.append(kb.s_ve.n)
    og_done = kb.s_ve.n

    state = {"mm": 0, "wo_at": [0, 0], "ve_bank": [0, 0, 0, 0]}

    def src_of(ob, tcx):
        par = ob % 2
        bank = ob % 4
        wos = wo_f[:, par * 4096:(par + 1) * 4096]
        if state["wo_at"][par]:
            g.wait_ge(kb.s_mm.h, state["wo_at"][par])
        swo[par].inc(g.dma_start(out=wos, in_=wop[ob]), 16)
        swo[par].wait(t)
        if ob == 0 and tcx == 0:
            t.wait_ge(kb.s_ve.h, og_done)
        if state["ve_bank"][bank]:
            t.wait_ge(kb.s_ve.h, state["ve_bank"][bank])
        for kk in range(32):
            ii = t.matmul(kb.ps[bank][:, :],
                          wos[:, kk * 128:(kk + 1) * 128],
                          o_loc[:, kk * TL + tcx * 512:
                                kk * TL + tcx * 512 + 512],
                          start=(kk == 0), stop=(kk == 31))
        kb.s_mm.inc(ii)
        state["wo_at"][par] = kb.s_mm.n
        state["ve_bank"][bank] = kb.s_ve.n + 1
        return kb.ps[bank][:, :], [(kb.s_mm, kb.s_mm.n)]

    emit_postln(kb, src_of, x_read, x_writes, last=last,
                y_of=lambda ob, tcx: ep_y_l0a[:, ob * 512:(ob + 1) * 512],
                xin_buf=ep_xin_l0a,
                aux={"y2": y2_l0a, "stat": stat_l0a,
                     "mub": mub_l0a, "rsb": rsb_l0a})



def emit_l1_attn(kb, x_read, x_writes, last=False):
    t, v, s, g, sy = kb.t, kb.v, kb.s, kb.g, kb.sy
    xb = kb.sb["xb"]
    CH = 256
    wqp = kb.inp("l1a_wq", [8, 128, HID])
    wkp = kb.inp("l1a_wk", [8, 128, HID])
    wsp = kb.inp("l1a_ws", [8, 128, HID])
    wvp = kb.inp("l1a_wv", [4, 128, NB * 512])
    wgp = kb.inp("l1a_wg", [NB, 128, HID])
    wop = kb.inp("l1a_wo", [NB, 128, HID])
    gnw = kb.inp("l1a_gnw", [1, 128])

    qk1_in = kb.scratch("l1a_qkstage", [2048, TL], BF16)
    qk1_out = kb.scratch("l1a_qkgat", [8192, TL], BF16)
    s1_in = kb.scratch("l1a_sstage", [TL, TL], F32)
    s1_out = kb.scratch("l1a_sgat", [4096, TL], F32)
    v1_in = kb.scratch("l1a_vstage", [TL, 2048], BF16)
    v1_out = kb.scratch("l1a_vgat", [4096, 2048], BF16)
    o1_in = kb.scratch("l1a_ostage", [512, T], BF16)
    o1_oS = [kb.scratch(f"l1a_og{i}", [512, T], BF16) for i in range(4)]
    g1park = kb.scratch("l1a_gpark", [2048, TL], BF16)

    AR, ARF = kb.AR, kb.ARF
    q12 = AR[:, 0:4096]
    k12 = AR[:, 4096:8192]
    v12 = AR[:, 8192:16384]
    ET_t = AR[:, 16384:20480]
    oT = AR[:, 20480:24576]
    s12 = ARF[:, 13664:17760]
    wst = AR[:, 0:4096]
    wvs = AR[:, 4096:12288]
    stb = AR[:, 12288:16384]
    sstg = ARF[:, 13664:15712]
    o_loc1 = AR[:, 0:16384]
    sg1 = AR[:, 16384:18432]
    gf1 = AR[:, 18432:20480]
    wo1 = AR[:, 20480:24576]
    Eb = AR[:, 24576:24704]
    ktm = AR[:, 24704:24832]
    A_sb = AR[:, 24832:25344]
    U_sb = AR[:, 25344:25856]
    Wf = AR[:, 25856:25984]
    WT_t = AR[:, 25984:26240]
    onb = AR[:, 26240:26368]
    gnwb = AR[:, 26368:26496]
    gnr = AR[0:1, 26496:26624]
    HKs_t = AR[:, 26624:26688]
    HVs_t = AR[:, 26688:26816]
    y2_l1a = AR[:, 26816:27840]
    q12b = AR[:, 61824:65920]
    k12b = AR[:, 65920:70016]
    v12b = AR[:, 70016:78208]
    Ef = ARF[:, 17760:17888]
    rz = ARF[:, 17888:18016]
    rz8 = ARF[:, 18016:18144]
    zc = ARF[0:1, 18144:18208]
    csum = ARF[0:1, 18208:18272]
    HKf_t = ARF[:, 18272:18336]
    HVf_t = ARF[:, 18336:18464]
    okf = ARF[:, 18464:18592]
    pvb = ARF[:, 18592:18720]
    ovf = ARF[:, 18720:18976]
    ov2 = ARF[:, 18976:19104]
    msq = ARF[:, 19104:19112]
    ep_y_l1a = ARF[:, 19112:27304]
    ep_xin_l1a = ARF[:, 27304:28328]
    mub_l1a = ARF[:, 28328:28840]
    rsb_l1a = ARF[:, 28840:29352]
    stat_l1a = ARF[0:1, 29352:30888]

    kb.barrier()
    swp = [kb.sem("l1a_w0"), kb.sem("l1a_w1")]
    sst = [kb.sem("l1a_s0"), kb.sem("l1a_s1")]
    svr = kb.sem("l1a_vrd")
    svr2 = kb.sem("l1a_vr2")
    sob = kb.sem("l1a_ob")
    sol = kb.sem("l1a_ol")
    sgf = [kb.sem("l1a_g0"), kb.sem("l1a_g1")]
    swo = [kb.sem("l1a_wo0"), kb.sem("l1a_wo1")]

    ps7b = kb.ps[7][:, :].bitcast(BF16)

    mm_at = [0, 0]
    ve_at = [0, 0]
    st_at = [0, 0]

    def proj_fm(wpack, n_ob, dst, dst_row0):
        for ob in range(n_ob):
            par = ob % 2
            ws = wst[:, par * HID:(par + 1) * HID]
            if mm_at[par]:
                g.wait_ge(kb.s_mm.h, mm_at[par])
            swp[par].inc(g.dma_start(out=ws, in_=wpack[ob]), 16)
            swp[par].wait(t)
            pq = (kb.ps[4 * par + 0], kb.ps[4 * par + 1])
            if ve_at[par]:
                t.wait_ge(kb.s_ve.h, ve_at[par])
            for k in range(NB):
                for half in range(2):
                    ii = t.matmul(pq[half][:, :],
                                  ws[:, k * 128:(k + 1) * 128],
                                  xb[:, k * TL + half * 512:
                                     k * TL + half * 512 + 512],
                                  start=(k == 0), stop=(k == NB - 1))
                if k == NB - 1:
                    kb.s_mm.inc(ii)
            mm_at[par] = kb.s_mm.n
            stg = stb[:, par * TL:(par + 1) * TL]
            v.wait_ge(kb.s_mm.h, mm_at[par])
            if st_at[par]:
                v.wait_ge(sst[par].h, st_at[par])
            for half in range(2):
                kb.s_ve.inc(v.tensor_copy(
                    stg[:, half * 512:half * 512 + 512], pq[half][:, :]))
            ve_at[par] = kb.s_ve.n
            sy.wait_ge(kb.s_ve.h, ve_at[par])
            sst[par].inc(sy.dma_start(
                out=dst[dst_row0 + ob * 128: dst_row0 + (ob + 1) * 128, :],
                in_=stg), 16)
            st_at[par] = sst[par].n

    proj_fm(wqp, 8, qk1_in, 0)
    proj_fm(wkp, 8, qk1_in, 1024)
    g.wait_ge(sst[0].h, sst[0].n)
    g.wait_ge(sst[1].h, sst[1].n)
    cc_qk = kb.cc_sem()
    cc_qk.inc(g.collective_compute(
        "AllGather", ALU.bypass, replica_groups=GROUPS,
        ins=[qk1_in[:, :]], outs=[qk1_out[:, :]]), 1)

    for ob in range(8):
        par = ob % 2
        ws = wst[:, par * HID:(par + 1) * HID]
        if mm_at[par]:
            g.wait_ge(kb.s_mm.h, mm_at[par])
        swp[par].inc(g.dma_start(out=ws, in_=wsp[ob]), 16)
        swp[par].wait(t)
        pq = (kb.ps[4 * par + 0], kb.ps[4 * par + 1])
        if ve_at[par]:
            t.wait_ge(kb.s_ve.h, ve_at[par])
        for k in range(NB):
            for half in range(2):
                ii = t.matmul(pq[half][:, :],
                              ws[:, k * 128:(k + 1) * 128],
                              xb[:, k * TL + half * 512:
                                 k * TL + half * 512 + 512],
                              start=(k == 0), stop=(k == NB - 1))
            if k == NB - 1:
                kb.s_mm.inc(ii)
        mm_at[par] = kb.s_mm.n
        v.wait_ge(kb.s_mm.h, mm_at[par])
        if st_at[par]:
            v.wait_ge(sst[par].h, st_at[par])
        sstp = sstg[:, par * 1024:(par + 1) * 1024]
        for half in range(2):
            kb.s_ve.inc(v.tensor_scalar(
                sstp[:, half * 512:half * 512 + 512],
                pq[half][:, :], 32.0, -32.0, ALU.min, ALU.max))
        ve_at[par] = kb.s_ve.n
        sy.wait_ge(kb.s_ve.h, ve_at[par])
        sst[par].inc(sy.dma_start(
            out=s1_in[ob * 128:(ob + 1) * 128, :], in_=sstp), 16)
        st_at[par] = sst[par].n

    g.wait_ge(sst[0].h, sst[0].n)
    g.wait_ge(sst[1].h, sst[1].n)
    cc_s = kb.cc_sem()
    cc_s.inc(g.collective_compute(
        "AllGather", ALU.bypass, replica_groups=GROUPS,
        ins=[s1_in[:, :]], outs=[s1_out[:, :]]), 1)

    t.wait_ge(kb.s_ve.h, kb.s_ve.n)
    vv_prev = 0
    for ch in range(4):
        g.wait_ge(kb.s_mm.h, kb.s_mm.n)
        swp[0].inc(g.dma_start(out=wvs, in_=wvp[ch]), 16)
        swp[0].wait(t)
        if ch > 0:
            t.wait_ge(kb.s_ve.h, vv_prev)
        for k in range(NB):
            for tb in range(8):
                ii = t.matmul(kb.ps[tb][:, :],
                              xb[:, k * TL + tb * 128:
                                 k * TL + tb * 128 + 128],
                              wvs[:, k * 512:(k + 1) * 512],
                              start=(k == 0), stop=(k == NB - 1))
            kb.s_mm.inc(ii)
        v.wait_ge(kb.s_mm.h, kb.s_mm.n)
        v.wait_ge(sst[0].h, sst[0].n)
        v.wait_ge(sst[1].h, sst[1].n)
        for tb in range(8):
            kb.s_ve.inc(v.tensor_copy(stb[:, tb * 512:(tb + 1) * 512],
                                      kb.ps[tb][:, :]))
        vv_prev = kb.s_ve.n
        sy.wait_ge(kb.s_ve.h, vv_prev)
        sst[0].inc(sy.dma_start(
            out=v1_in.rearrange("(tb p) c -> p tb c", p=128)[
                :, :, ch * 512:(ch + 1) * 512],
            in_=stb.rearrange("p (tb c) -> p tb c", tb=8)), 16)

    g.wait_ge(sst[0].h, sst[0].n)
    mm_vl = kb.s_mm.n
    cc_v = kb.cc_sem()
    cc_v.inc(g.collective_compute(
        "AllGather", ALU.bypass, replica_groups=GROUPS,
        ins=[v1_in[:, :]], outs=[v1_out[:, :]]), 1)

    qk1_r = qk1_out.rearrange("(r4 rows) t -> rows r4 t", rows=2048)
    s1_r = s1_out.rearrange("(r4 rows) t -> rows r4 t", rows=1024)
    v1_r3 = v1_out.rearrange("(tb p) c -> p tb c", p=128)
    s.wait_ge(cc_qk.h, 1)
    s.wait_ge(cc_s.h, 1)
    s.wait_ge(kb.s_mm.h, mm_vl)
    s.wait_ge(sst[0].h, sst[0].n)
    s.wait_ge(sst[1].h, sst[1].n)
    pid_z = s.partition_id()
    qrow0 = (pid_z % 4) * 256
    svr.inc(kb.dma(s, out=k12.rearrange("p (r4 t) -> p r4 t", r4=4),
                   in_=qk1_r[bass.ds(1024 + qrow0, 128)]), 16)
    svr.inc(kb.dma(s, out=s12.rearrange("p (r4 t) -> p r4 t", r4=4),
                   in_=s1_r[bass.ds(qrow0, 128)]), 16)

    mv = kb.s_ve.n
    ve_at = [mv, mv]
    v.wait_ge(sst[0].h, sst[0].n)
    st_at = [0, 0]
    proj_fm(wgp, NB, g1park, 0)
    mm_gp = kb.s_mm.n
    s.wait_ge(cc_v.h, 1)
    s.wait_ge(kb.s_mm.h, mm_gp)
    s.wait_ge(sst[0].h, sst[0].n)
    s.wait_ge(sst[1].h, sst[1].n)
    svr.inc(kb.dma(s, out=v12.rearrange("p (tb c) -> p tb c", c=256),
                   in_=v1_r3[:, :, bass.ds((pid_z % 4) * 512, 256)]), 16)
    svr.inc(kb.dma(s, out=q12.rearrange("p (r4 t) -> p r4 t", r4=4),
                   in_=qk1_r[bass.ds(qrow0, 128)]), 16)

    sob.inc(g.dma_start(out=gnr[:, :], in_=gnw), 16)
    sob.wait(t)
    t.wait_ge(kb.s_ve.h, kb.s_ve.n)
    kb.s_mm.inc(t.matmul(kb.ps[6][:, 0:128], kb.ones_row(BF16), gnr[0:1, :],
                         start=True, stop=True))
    v.wait_ge(kb.s_mm.h, kb.s_mm.n)
    kb.s_ve.inc(v.tensor_copy(gnwb[:, :], kb.ps[6][:, 0:128]))

    idf = kb.cs(C_IDENT, F32)
    idb = kb.cs(C_IDENT, BF16)
    trilA_f = kb.cs(C_TRILA, F32)[:, 0:128]
    trilA_b = kb.cs(C_TRILA, BF16)[:, 0:128]
    ones_f = kb.cs(C_ONES, F32)

    cc_os = []
    for pg_i in range(2):
        if pg_i == 0:
            svr.wait(t)
            svr.wait(s)
            svr.wait(v)
            g.wait_ge(cc_qk.h, 1)
            g.wait_ge(cc_v.h, 1)
            for i4 in nc_switch4(kb):
                qrow1 = i4 * 256 + 128
                g.dma_start(
                    out=q12b.rearrange("p (r4 t) -> p r4 t", r4=4),
                    in_=qk1_r[qrow1:qrow1 + 128]).then_inc(svr2.h, 16)
                g.dma_start(
                    out=k12b.rearrange("p (r4 t) -> p r4 t", r4=4),
                    in_=qk1_r[1024 + qrow1:1024 + qrow1 + 128]).then_inc(
                        svr2.h, 16)
                colv1 = i4 * 512 + 256
                g.dma_start(
                    out=v12b.rearrange("p (tb c) -> p tb c", c=256),
                    in_=v1_r3[:, :, colv1:colv1 + 256]).then_inc(svr2.h, 16)
            svr2.n += 3 * 16
            q12c, k12c, v12c = q12, k12, v12
        else:
            svr2.wait(t)
            svr2.wait(s)
            svr2.wait(v)
            q12c, k12c, v12c = q12b, k12b, v12b
        for pp in range(2):
            po = pp * 64
            pv_off = pp * 128
            HKf = HKf_t[po:po + 64, :]
            HVf = HVf_t[po:po + 64, :]
            HKs = HKs_t[po:po + 64, :]
            HVs = HVs_t[po:po + 64, :]
            WT = WT_t[po:po + 64, :]
            ETp = ET_t[po:po + 64, :]
            s.wait_ge(kb.s_mm.h, kb.s_mm.n)
            kb.s_ac.inc(s.activation(ETp[:, :], s12[po:po + 64, :], AF.Exp))
            if pg_i == 0 and pp == 1:
                g.wait_ge(kb.s_ac.h, kb.s_ac.n)
                g.wait_ge(cc_s.h, 1)
                for i4 in nc_switch4(kb):
                    qrow1 = i4 * 256 + 128
                    g.dma_start(
                        out=s12.rearrange("p (r4 t) -> p r4 t", r4=4),
                        in_=s1_r[qrow1:qrow1 + 128]).then_inc(svr2.h, 16)
                svr2.n += 16
            v.wait_ge(sst[0].h, sst[0].n)
            kb.s_ve.inc(v.memset(zc[:, :], 0.0))
            kb.s_ve.inc(v.memset(HKf[:, :], 0.0))
            kb.s_ve.inc(v.memset(HVf[:, :], 0.0))
            for c in range(16):
                ps6b = kb.ps[6][:, :].bitcast(BF16)
                for tb2 in range(2):
                    t.wait_ge(kb.s_ac.h, kb.s_ac.n)
                    t.wait_ge(kb.s_ve.h, kb.s_ve.n)
                    kb.s_mm.inc(t.transpose(
                        ps6b[:, 0:64],
                        ET_t[po:po + 64,
                             c * CH + tb2 * 128: c * CH + tb2 * 128 + 128],
                        idb[po:po + 64, po:po + 64]))
                    v.wait_ge(kb.s_mm.h, kb.s_mm.n)
                    kb.s_ve.inc(v.tensor_copy(
                        Eb[:, tb2 * 64:(tb2 + 1) * 64], ps6b[:, 0:64]))
                    kb.s_ve.inc(v.tensor_copy(
                        Ef[:, tb2 * 64:(tb2 + 1) * 64], ps6b[:, 0:64]))
                    kb.s_mm.inc(t.transpose(
                        ps7b[:, 0:64],
                        k12c[po:po + 64,
                             c * CH + tb2 * 128: c * CH + tb2 * 128 + 128],
                        idb[po:po + 64, po:po + 64]))
                    v.wait_ge(kb.s_mm.h, kb.s_mm.n)
                    kb.s_ve.inc(v.tensor_copy(
                        ktm[:, tb2 * 64:(tb2 + 1) * 64], ps7b[:, 0:64]))
                    t.wait_ge(kb.s_ve.h, kb.s_ve.n)
                t.wait_ge(kb.s_ve.h, kb.s_ve.n)
                for tb2 in range(2):
                    pz = kb.ps[4 + tb2]
                    t.matmul(pz[:, 0:64], trilA_f,
                             Ef[:, tb2 * 64:tb2 * 64 + 64],
                             start=True, stop=False)
                    if tb2 == 1:
                        t.matmul(pz[:, 0:64], ones_f[:, 0:128], Ef[:, 0:64],
                                 start=False, stop=False)
                    ii = t.matmul(pz[:, 0:64], kb.ones_row(F32), zc[0:1, :],
                                  start=False, stop=True)
                    kb.s_mm.inc(ii)
                v.wait_ge(kb.s_mm.h, kb.s_mm.n)
                for tb2 in range(2):
                    kb.s_ve.inc(v.reciprocal(rz[:, tb2 * 64:(tb2 + 1) * 64],
                                             kb.ps[4 + tb2][:, 0:64]))
                kb.vsync()
                kb.s_ve.inc(v.tensor_scalar(rz8[:, :], rz[:, :], 0.125,
                                            None, ALU.mult))
                t.matmul(kb.ps[6][0:1, 0:64], kb.ones_col(F32), Ef[:, 0:64],
                         start=True, stop=False)
                ii = t.matmul(kb.ps[6][0:1, 0:64], kb.ones_col(F32),
                              Ef[:, 64:128], start=False, stop=True)
                kb.s_mm.inc(ii)
                v.wait_ge(kb.s_mm.h, kb.s_mm.n)
                kb.s_ve.inc(v.tensor_copy(csum[:, :], kb.ps[6][0:1, 0:64]))
                kb.vsync()
                kb.s_ve.inc(v.tensor_add(zc[:, :], zc[:, :], csum[:, :]))
                for jb in range(2):
                    ii = t.matmul(kb.ps[2 + jb][:, 0:256],
                                  k12c[po:po + 64,
                                       c * CH + jb * 128:
                                       c * CH + jb * 128 + 128],
                                  q12c[po:po + 64, c * CH:(c + 1) * CH],
                                  start=True, stop=True)
                    kb.s_mm.inc(ii)
                v.wait_ge(kb.s_mm.h, kb.s_mm.n)
                kb.s_ve.inc(v.tensor_mul(A_sb[:, 0:128], kb.ps[2][:, 0:128],
                                         trilA_b))
                kb.s_ve.inc(v.tensor_copy(A_sb[:, 128:256],
                                          kb.ps[2][:, 128:256]))
                kb.s_ve.inc(v.memset(A_sb[:, 256:384], 0.0))
                kb.s_ve.inc(v.tensor_mul(A_sb[:, 384:512],
                                         kb.ps[3][:, 128:256], trilA_b))
                kb.s_ve.inc(v.tensor_copy(HKs[:, :], HKf[:, :]))
                kb.s_ve.inc(v.tensor_copy(HVs[:, :], HVf[:, :]))
                snap_ve = kb.s_ve.n
                t.wait_ge(kb.s_ve.h, snap_ve)
                for jb in range(2):
                    i1 = t.matmul(kb.ps[0][0:64, 0:64],
                                  ktm[:, jb * 64:(jb + 1) * 64],
                                  Eb[:, jb * 64:(jb + 1) * 64],
                                  start=(jb == 0), stop=(jb == 1))
                    i2 = t.matmul(kb.ps[1][0:64, 0:128],
                                  Eb[:, jb * 64:(jb + 1) * 64],
                                  v12c[:, (c * 2 + jb) * 256 + pv_off:
                                       (c * 2 + jb) * 256 + pv_off + 128],
                                  start=(jb == 0), stop=(jb == 1))
                kb.s_mm.inc(i1)
                kb.s_mm.inc(i2)
                hk_mm = kb.s_mm.n
                v.wait_ge(kb.s_mm.h, hk_mm)
                kb.s_ve.inc(v.tensor_add(HKf[:, :], kb.ps[0][0:64, 0:64],
                                         HKf[:, :]))
                kb.s_ve.inc(v.tensor_add(HVf[:, :], kb.ps[1][0:64, 0:128],
                                         HVf[:, :]))
                t.wait_ge(kb.s_ve.h, kb.s_ve.n)
                for tb2 in range(2):
                    pok = kb.ps[4 + tb2]
                    t.matmul(pok[:, 0:64],
                             A_sb[:, tb2 * 128:tb2 * 128 + 128],
                             Eb[:, 0:64], start=True, stop=False)
                    t.matmul(pok[:, 0:64],
                             A_sb[:, 256 + tb2 * 128:256 + tb2 * 128 + 128],
                             Eb[:, 64:128], start=False, stop=False)
                    ii = t.matmul(pok[:, 0:64],
                                  q12c[po:po + 64,
                                       c * CH + tb2 * 128:
                                       c * CH + tb2 * 128 + 128],
                                  HKs[:, :], start=False, stop=True)
                    kb.s_mm.inc(ii)
                v.wait_ge(kb.s_mm.h, kb.s_mm.n)
                for tb2 in range(2):
                    kb.s_ve.inc(v.tensor_tensor(
                        okf[:, tb2 * 64:(tb2 + 1) * 64],
                        kb.ps[4 + tb2][:, 0:64],
                        rz8[:, tb2 * 64:(tb2 + 1) * 64], ALU.mult))
                kb.vsync()
                nm = msq[:, 0:1]
                sm = msq[:, 1:2]
                for tb2 in range(2):
                    ok2 = okf[:, tb2 * 64:(tb2 + 1) * 64]
                    kb.s_ve.inc(v.reduce_max(nm, ok2, mybir.AxisListType.X,
                                             negate=True))
                    kb.vsync()
                    s.wait_ge(kb.s_ve.h, kb.s_ve.n)
                    kb.s_ac.inc(s.activation(pvb[:, tb2 * 64:(tb2 + 1) * 64],
                                             ok2, AF.Exp, bias=nm,
                                             accum_out=sm))
                    v.wait_ge(kb.s_ac.h, kb.s_ac.n)
                    kb.s_ve.inc(v.reciprocal(sm, sm))
                    kb.vsync()
                    kb.s_ve.inc(v.tensor_scalar(
                        pvb[:, tb2 * 64:(tb2 + 1) * 64],
                        pvb[:, tb2 * 64:(tb2 + 1) * 64], sm, None, ALU.mult))
                    kb.vsync()
                kb.s_ve.inc(v.tensor_tensor(Wf[:, :], pvb[:, :], rz[:, :],
                                            ALU.mult))
                kb.vsync()
                for tb2 in range(2):
                    t.wait_ge(kb.s_ve.h, kb.s_ve.n)
                    kb.s_mm.inc(t.transpose(
                        ps7b[0:64, 0:128],
                        Wf[:, tb2 * 64:(tb2 + 1) * 64], idb))
                    v.wait_ge(kb.s_mm.h, kb.s_mm.n)
                    kb.s_ve.inc(v.tensor_copy(
                        WT[:, tb2 * 128:(tb2 + 1) * 128],
                        ps7b[0:64, 0:128]))
                    t.wait_ge(kb.s_ve.h, kb.s_ve.n)
                for jb in range(2):
                    ii = t.matmul(kb.ps[2 + jb][:, 0:256],
                                  ETp[:, c * CH + jb * 128:
                                      c * CH + jb * 128 + 128],
                                  WT[:, 0:256],
                                  start=True, stop=True)
                    kb.s_mm.inc(ii)
                v.wait_ge(kb.s_mm.h, kb.s_mm.n)
                kb.s_ve.inc(v.tensor_mul(U_sb[:, 0:128], kb.ps[2][:, 0:128],
                                         trilA_b))
                kb.s_ve.inc(v.tensor_copy(U_sb[:, 128:256],
                                          kb.ps[2][:, 128:256]))
                kb.s_ve.inc(v.memset(U_sb[:, 256:384], 0.0))
                kb.s_ve.inc(v.tensor_mul(U_sb[:, 384:512],
                                         kb.ps[3][:, 128:256], trilA_b))
                tve_U = kb.s_ve.n
                t.wait_ge(kb.s_ve.h, tve_U)
                for tb2 in range(2):
                    pov = kb.ps[4 + tb2]
                    t.matmul(pov[:, 0:128],
                             U_sb[:, tb2 * 128:tb2 * 128 + 128],
                             v12c[:, (c * 2 + 0) * 256 + pv_off:
                                  (c * 2 + 0) * 256 + pv_off + 128],
                             start=True, stop=False)
                    t.matmul(pov[:, 0:128],
                             U_sb[:, 256 + tb2 * 128:256 + tb2 * 128 + 128],
                             v12c[:, (c * 2 + 1) * 256 + pv_off:
                                  (c * 2 + 1) * 256 + pv_off + 128],
                             start=False, stop=False)
                    ii = t.matmul(pov[:, 0:128],
                                  WT[:, tb2 * 128:(tb2 + 1) * 128],
                                  HVs[:, :], start=False, stop=True)
                    kb.s_mm.inc(ii)
                mm_ov = kb.s_mm.n
                v.wait_ge(kb.s_mm.h, mm_ov)
                for tb2 in range(2):
                    ovb = ovf[:, tb2 * 128:(tb2 + 1) * 128]
                    kb.s_ve.inc(v.tensor_copy(ovb, kb.ps[4 + tb2][:, 0:128]))
                    kb.vsync()
                    ms = msq[:, 2:3]
                    kb.s_ve.inc(v.scalar_tensor_tensor(
                        ov2[:, :], ovb, 1.0, ovb, ALU.mult, ALU.mult,
                        accum_out=ms))
                    kb.vsync()
                    kb.s_ve.inc(v.tensor_scalar(ms, ms, 1.0 / 128, 1e-5,
                                                ALU.mult, ALU.add))
                    kb.vsync()
                    s.wait_ge(kb.s_ve.h, kb.s_ve.n)
                    kb.s_ac.inc(s.activation(ms, ms, AF.Sqrt))
                    v.wait_ge(kb.s_ac.h, kb.s_ac.n)
                    kb.s_ve.inc(v.reciprocal(ms, ms))
                    kb.vsync()
                    kb.s_ve.inc(v.scalar_tensor_tensor(
                        onb[:, :], ovb, ms, gnwb[:, :], ALU.mult, ALU.mult))
                    kb.vsync()
                    t.wait_ge(kb.s_ve.h, kb.s_ve.n)
                    kb.s_mm.inc(t.transpose(ps7b[:, 0:128], onb[:, :], idb))
                    v.wait_ge(kb.s_mm.h, kb.s_mm.n)
                    kb.s_ve.inc(v.tensor_copy(
                        oT[:, c * CH + tb2 * 128: c * CH + tb2 * 128 + 128],
                        ps7b[:, 0:128]))
                    t.wait_ge(kb.s_ve.h, kb.s_ve.n)
            sy.wait_ge(kb.s_ve.h, kb.s_ve.n)
            pbk = pg_i * 2 + pp
            sst[0].inc(sy.dma_start(
                out=o1_in[pbk * 128: pbk * 128 + 128, :],
                in_=oT[:, :]), 16)
            g.wait_ge(sst[0].h, sst[0].n)
            cco = kb.cc_sem()
            cco.inc(g.collective_compute(
                "AllGather", ALU.bypass, replica_groups=GROUPS,
                ins=[o1_in[pbk * 128: pbk * 128 + 128, :]],
                outs=[o1_oS[pbk][:, :]]), 1)
            cc_os.append(cco)

    for cco in cc_os:
        g.wait_ge(cco.h, 1)
    kb.s_go.inc(g.engine_nop())

    g.wait_ge(kb.s_go.h, kb.s_go.n)
    g.wait_ge(kb.s_mm.h, kb.s_mm.n)
    g.wait_ge(kb.s_ve.h, kb.s_ve.n)
    for i4 in nc_switch4(kb):
        tok0 = i4 * TL
        for sN in range(4):
            g.dma_start(
                out=o_loc1.rearrange("p (r4 s4 t) -> p r4 s4 t",
                                     r4=4, s4=4)[:, :, sN],
                in_=o1_oS[sN].rearrange("(r4 p) t -> p r4 t", p=128)[
                    :, :, tok0:tok0 + TL]).then_inc(sol.h, 16)
    sol.n += 4 * 16
    gate_ve = []
    for pb in range(NB):
        par = pb % 2
        gf = gf1[:, par * TL:(par + 1) * TL]
        if pb >= 2:
            sy.wait_ge(kb.s_ac.h, kb.s_ac.n)
        sgf[par].inc(sy.dma_start(
            out=gf, in_=g1park[pb * 128:(pb + 1) * 128, :]), 16)
        sgf[par].wait(s)
        sgg = sg1[:, par * TL:(par + 1) * TL]
        if pb >= 2:
            s.wait_ge(kb.s_ve.h, gate_ve[pb - 2])
        kb.s_ac.inc(s.activation(sgg, gf, AF.Silu))
        v.wait_ge(kb.s_ac.h, kb.s_ac.n)
        if pb == 0:
            v.wait_ge(sol.h, sol.n)
        kb.s_ve.inc(v.tensor_tensor(o_loc1[:, pb * TL:(pb + 1) * TL],
                                    o_loc1[:, pb * TL:(pb + 1) * TL],
                                    sgg, ALU.mult))
        gate_ve.append(kb.s_ve.n)
    og_done = kb.s_ve.n

    state = {"wo_at": [0, 0], "ve_bank": [0, 0, 0, 0]}

    def src_of(ob, tcx):
        par = ob % 2
        bank = ob % 4
        wos = wo1[:, par * HID:(par + 1) * HID]
        if state["wo_at"][par]:
            g.wait_ge(kb.s_mm.h, state["wo_at"][par])
        swo[par].inc(g.dma_start(out=wos, in_=wop[ob]), 16)
        swo[par].wait(t)
        if ob == 0 and tcx == 0:
            t.wait_ge(kb.s_ve.h, og_done)
        if state["ve_bank"][bank]:
            t.wait_ge(kb.s_ve.h, state["ve_bank"][bank])
        for kk in range(NB):
            ii = t.matmul(kb.ps[bank][:, :],
                          wos[:, kk * 128:(kk + 1) * 128],
                          o_loc1[:, kk * TL + tcx * 512:
                                 kk * TL + tcx * 512 + 512],
                          start=(kk == 0), stop=(kk == NB - 1))
        kb.s_mm.inc(ii)
        state["wo_at"][par] = kb.s_mm.n
        state["ve_bank"][bank] = kb.s_ve.n + 1
        return kb.ps[bank][:, :], [(kb.s_mm, kb.s_mm.n)]

    emit_postln(kb, src_of, x_read, x_writes, last=last,
                y_of=lambda ob, tcx: ep_y_l1a[:, ob * TC:(ob + 1) * TC],
                xin_buf=ep_xin_l1a,
                aux={"y2": y2_l1a, "stat": stat_l1a,
                     "mub": mub_l1a, "rsb": rsb_l1a})

def emit_l1_ffn(kb, x_read, x_writes, last=False):
    t, v, s, g, sy = kb.t, kb.v, kb.s, kb.g, kb.sy
    xb = kb.sb["xb"]
    HB1 = 11
    gw = kb.inp("l1f_gw", [128, NB * 8])
    ebias = kb.inp("l1f_eb", [1, 8])
    w1 = kb.inp("l1f_w1", [8, HB1, 128, HID])
    w3 = kb.inp("l1f_w3", [8, HB1, 128, HID])
    w2 = kb.inp("l1f_w2", [8, NB, 128, 1408])
    sw1 = kb.inp("l1f_sw1", [HB1, 128, HID])
    sw3 = kb.inp("l1f_sw3", [HB1, 128, HID])
    sw2 = kb.inp("l1f_sw2", [NB, 128, 1408])

    AR, ARF = kb.AR, kb.ARF
    accd = kb.scratch("l1f_accd", [HID, TL], F32)
    xe = AR[:, 0:16384]
    h1 = AR[:, 16384:27648]
    sg1 = AR[:, 27648:29696]
    wa_sb = AR[:, 29696:33792]
    wb_sb = AR[:, 33792:37888]
    wc_sb = AR[:, 37888:40704]
    wa_slots = [AR[:, 29696 + i * 2048: 29696 + (i + 1) * 2048]
                for i in range(2)] + \
               [AR[:, 53248 + i * 2048: 53248 + (i + 1) * 2048]
                for i in range(2)]
    wb_slots = [AR[:, 33792 + i * 2048: 33792 + (i + 1) * 2048]
                for i in range(2)] + \
               [AR[:, 57344 + i * 2048: 57344 + (i + 1) * 2048]
                for i in range(2)]
    wc_slots = [AR[:, 37888 + i * 1408: 37888 + (i + 1) * 1408]
                for i in range(2)] + \
               [AR[:, 61440 + i * 1408: 61440 + (i + 1) * 1408]
                for i in range(2)]
    wbc = AR[:, 40704:41728]
    wrows = AR[0:8, 41728:42752]
    wrow = AR[0:1, 42752:43776]
    gw_sb2 = ARF[:, 22144:22272]
    xf = ARF[:, 22272:23296]
    sc_sb = ARF[:, 23296:23360]
    bb_sb = ARF[:, 23360:23368]
    t8 = ARF[:, 23368:23416]
    ebr = ARF[0:1, 23416:23424]
    we_sb = ARF[:, 23424:23488]
    dstg = ARF[:, 23552:25600]
    srcb = ARF[:, 25600:26624]
    ep_y1 = ARF[:, 26624:34816]
    ep_xin1 = ARF[:, 34816:35840]
    ep_aux1 = {"y2": AR[:, 43776:44800], "stat": ARF[0:1, 35840:37376],
               "mub": ARF[:, 37376:37888], "rsb": ARF[:, 37888:38400]}

    kb.barrier()
    sacc = [kb.sem("l1f_a0"), kb.sem("l1f_a1")]
    sdl = [kb.sem("l1f_d0"), kb.sem("l1f_d1")]
    sxf = [kb.sem("l1f_x0"), kb.sem("l1f_x1")]
    swp = [kb.sem(f"l1f_wp{i}") for i in range(4)]
    sw2p = [kb.sem(f"l1f_wq{i}") for i in range(4)]
    sgw = kb.sem("l1f_gws")
    srow = kb.sem("l1f_row")

    gw_sb = gw_sb2
    sgw.inc(sy.dma_start(out=gw_sb[:, :], in_=gw), 16)
    sgw.inc(sy.dma_start(out=ebr[:, :], in_=ebias), 16)
    cc_i = 0
    for k in range(NB):
        for hf in range(2):
            par = cc_i % 2
            cc_i += 1
            xfb = xf[:, par * TC:(par + 1) * TC]
            if cc_i > 2:
                sy.wait_ge(kb.s_mm.h, kb.s_mm.n)
            sxf[par].inc(sy.dma_start(
                out=xfb, in_=x_read[k * 128:(k + 1) * 128,
                                    hf * TC:(hf + 1) * TC]), 16)
            sxf[par].wait(t)
            if k == 0:
                sgw.wait(t)
            for tl4 in range(4):
                tile = hf * 4 + tl4
                ii = t.matmul(kb.ps[tile][:, 0:8],
                              xfb[:, tl4 * 128:(tl4 + 1) * 128],
                              gw_sb[:, k * 8:(k + 1) * 8],
                              start=(k == 0), stop=(k == NB - 1))
            kb.s_mm.inc(ii)
    sc_mm = kb.s_mm.n
    s.wait_ge(kb.s_mm.h, sc_mm)
    for tile in range(8):
        kb.s_ac.inc(s.activation(sc_sb[:, tile * 8:(tile + 1) * 8],
                                 kb.ps[tile][:, 0:8], AF.Sigmoid))
    t.wait_ge(kb.s_ac.h, kb.s_ac.n)
    kb.s_mm.inc(t.matmul(kb.ps[5][:, 0:8], kb.ones_row(F32), ebr[0:1, :],
                         start=True, stop=True))
    v.wait_ge(kb.s_mm.h, kb.s_mm.n)
    kb.s_ve.inc(v.tensor_copy(bb_sb[:, :], kb.ps[5][:, 0:8]))
    v.wait_ge(kb.s_ac.h, kb.s_ac.n)
    iom = kb.cs(C_IOTA8M, F32)
    for tile in range(8):
        sct = sc_sb[:, tile * 8:(tile + 1) * 8]
        bsc = t8[:, 0:8]
        m1 = t8[:, 8:9]
        eq = t8[:, 16:24]
        tt1 = t8[:, 24:32]
        i1 = t8[:, 9:10]
        oh1 = t8[:, 32:40]
        ohh = t8[:, 40:48]
        kb.vsync()
        kb.s_ve.inc(v.tensor_add(bsc, sct, bb_sb[:, :]))
        kb.vsync()
        kb.s_ve.inc(v.reduce_max(m1, bsc, mybir.AxisListType.X))
        kb.vsync()
        kb.s_ve.inc(v.tensor_single_scalar(eq, bsc, m1, ALU.is_equal))
        kb.vsync()
        kb.s_ve.inc(v.tensor_mul(tt1, eq, iom))
        kb.vsync()
        kb.s_ve.inc(v.tensor_reduce(i1, tt1, mybir.AxisListType.X, ALU.min))
        kb.vsync()
        kb.s_ve.inc(v.tensor_single_scalar(oh1, iom, i1, ALU.is_equal))
        kb.vsync()
        kb.s_ve.inc(v.scalar_tensor_tensor(bsc, oh1, -1e9, bsc,
                                           ALU.mult, ALU.add))
        kb.vsync()
        kb.s_ve.inc(v.reduce_max(m1, bsc, mybir.AxisListType.X))
        kb.vsync()
        kb.s_ve.inc(v.tensor_single_scalar(eq, bsc, m1, ALU.is_equal))
        kb.vsync()
        kb.s_ve.inc(v.tensor_mul(tt1, eq, iom))
        kb.vsync()
        kb.s_ve.inc(v.tensor_reduce(i1, tt1, mybir.AxisListType.X, ALU.min))
        kb.vsync()
        kb.s_ve.inc(v.tensor_single_scalar(eq, iom, i1, ALU.is_equal))
        kb.vsync()
        kb.s_ve.inc(v.tensor_add(ohh, oh1, eq))
        kb.vsync()
        kb.s_ve.inc(v.tensor_mul(we_sb[:, tile * 8:(tile + 1) * 8],
                                 sct, ohh))
    idf = kb.cs(C_IDENT, F32)
    for tile in range(8):
        t.wait_ge(kb.s_ve.h, kb.s_ve.n)
        kb.s_mm.inc(t.transpose(kb.ps[6][0:8, 0:128],
                                we_sb[:, tile * 8:(tile + 1) * 8], idf))
        v.wait_ge(kb.s_mm.h, kb.s_mm.n)
        kb.s_ve.inc(v.tensor_copy(wrows[0:8, tile * 128:(tile + 1) * 128],
                                  kb.ps[6][0:8, 0:128]))
        t.wait_ge(kb.s_ve.h, kb.s_ve.n)

    def run_expert(w1p, w3p, w2p, rhs_buf, add, prep=None):
        t.wait_ge(kb.s_ve.h, kb.s_ve.n)
        mm_at = [0, 0, 0, 0]
        ac_at = [0, 0]
        ve_at = [0, 0]
        for hb in range(HB1):
            p4 = hb % 4
            par = hb % 2
            was = wa_slots[p4]
            wbs = wb_slots[p4]
            if mm_at[p4]:
                g.wait_ge(kb.s_mm.h, mm_at[p4])
            swp[p4].inc(g.dma_start(out=was, in_=w1p[hb]), 16)
            swp[p4].inc(g.dma_start(out=wbs, in_=w3p[hb]), 16)
            pg = (kb.ps[4 * par + 0], kb.ps[4 * par + 1])
            pu = (kb.ps[4 * par + 2], kb.ps[4 * par + 3])
            swp[p4].wait(t)
            if ac_at[par]:
                t.wait_ge(kb.s_ac.h, ac_at[par])
                t.wait_ge(kb.s_ve.h, ve_at[par])
            for half in range(2):
                for k in range(NB):
                    rhs = rhs_buf[:, k * TL + half * 512:
                                  k * TL + half * 512 + 512]
                    st, sp = (k == 0), (k == NB - 1)
                    i1 = t.matmul(pg[half][:, :],
                                  was[:, k * 128:(k + 1) * 128], rhs,
                                  start=st, stop=sp)
                    i2 = t.matmul(pu[half][:, :],
                                  wbs[:, k * 128:(k + 1) * 128], rhs,
                                  start=st, stop=sp)
                    if sp:
                        kb.s_mm.inc(i1)
                        kb.s_mm.inc(i2)
            mm_at[p4] = kb.s_mm.n
            hs = h1[:, hb * TL:(hb + 1) * TL]
            sgs = sg1[:, par * TL:(par + 1) * TL]
            s.wait_ge(kb.s_mm.h, mm_at[p4])
            for half in range(2):
                kb.s_ac.inc(s.activation(sgs[:, half * 512:half * 512 + 512],
                                         pg[half][:, :], AF.Silu))
            ac_at[par] = kb.s_ac.n
            v.wait_ge(kb.s_ac.h, ac_at[par])
            for half in range(2):
                kb.s_ve.inc(v.tensor_tensor(
                    hs[:, half * 512:half * 512 + 512], pu[half][:, :],
                    sgs[:, half * 512:half * 512 + 512], ALU.mult))
            ve_at[par] = kb.s_ve.n
        mm_gu = kb.s_mm.n
        ac_gu = kb.s_ac.n
        ve_gu = kb.s_ve.n
        if prep is not None:
            prep(mm_gu)
        t.wait_ge(kb.s_ac.h, ac_gu)
        t.wait_ge(kb.s_ve.h, ve_gu)
        dmm = [0, 0, 0, 0]
        dve = [0, 0]
        for ob in range(NB):
            p4 = ob % 4
            par = ob % 2
            wcs = wc_slots[p4]
            if dmm[p4]:
                g.wait_ge(kb.s_mm.h, dmm[p4])
            sw2p[p4].inc(g.dma_start(out=wcs, in_=w2p[ob]), 16)
            pd = (kb.ps[4 * par + 0], kb.ps[4 * par + 1])
            sw2p[p4].wait(t)
            if dve[par]:
                t.wait_ge(kb.s_ve.h, dve[par])
            for half in range(2):
                for k in range(HB1):
                    i1 = t.matmul(pd[half][:, :],
                                  wcs[:, k * 128:(k + 1) * 128],
                                  h1[:, k * TL + half * 512:
                                     k * TL + half * 512 + 512],
                                  start=(k == 0), stop=(k == HB1 - 1))
                    if k == HB1 - 1:
                        kb.s_mm.inc(i1)
            dmm[p4] = kb.s_mm.n
            v.wait_ge(kb.s_mm.h, dmm[p4])
            if sacc[par].n:
                v.wait_ge(sacc[par].h, sacc[par].n)
            ds_ = dstg[:, par * TL:(par + 1) * TL]
            for half in range(2):
                kb.s_ve.inc(v.tensor_copy(
                    ds_[:, half * 512:half * 512 + 512], pd[half][:, :]))
            dve[par] = kb.s_ve.n
            g.wait_ge(kb.s_ve.h, dve[par])
            if sacc[par].n:
                g.wait_ge(sacc[par].h, sacc[par].n)
            sacc[par].inc(g.dma_start(
                out=accd[ob * 128:(ob + 1) * 128, :], in_=ds_,
                accum_op=(ALU.add if add else ALU.bypass)), 16)
        t.wait_ge(kb.s_ve.h, kb.s_ve.n)

    ve_wrows = kb.s_ve.n
    bst = {"mm_bc": 0}

    def mk_prep(e):
        def prep(mm_gu):
            if e == 0:
                g.wait_ge(kb.s_ve.h, ve_wrows)
            if bst["mm_bc"]:
                g.wait_ge(kb.s_mm.h, bst["mm_bc"])
            srow.inc(g.dma_start(out=wrow[:, :], in_=wrows[e:e + 1, :]), 16)
            srow.wait(t)
            t.wait_ge(kb.s_ve.h, kb.s_ve.n)
            t.wait_ge(kb.s_ac.h, kb.s_ac.n)
            for half in range(2):
                kb.s_mm.inc(t.matmul(kb.ps[4 + half][:, :],
                                     kb.ones_row(BF16),
                                     wrow[0:1, half * 512:half * 512 + 512],
                                     start=True, stop=True))
            bst["mm_bc"] = kb.s_mm.n
            v.wait_ge(kb.s_mm.h, bst["mm_bc"])
            for half in range(2):
                kb.s_ve.inc(v.tensor_copy(
                    wbc[:, half * 512:half * 512 + 512],
                    kb.ps[4 + half][:, :]))
            kb.vsync()
            v.wait_ge(kb.s_mm.h, mm_gu)
            for k in range(NB):
                kb.s_ve.inc(v.tensor_tensor(xe[:, k * TL:(k + 1) * TL],
                                            xb[:, k * TL:(k + 1) * TL],
                                            wbc[:, :], ALU.mult))
        return prep

    run_expert(sw1, sw3, sw2, xb, add=False, prep=mk_prep(0))
    for e in range(8):
        run_expert(w1[e], w3[e], w2[e], xe, add=True,
                   prep=mk_prep(e + 1) if e < 7 else None)

    g.wait_ge(sacc[0].h, sacc[0].n)
    g.wait_ge(sacc[1].h, sacc[1].n)
    kb.s_go.inc(g.engine_nop())
    sy.wait_ge(kb.s_go.h, kb.s_go.n)
    ve_src = [0, 0]

    def src_of(ob, tcx):
        par = ob % 2
        sb_ = srcb[:, par * TC:(par + 1) * TC]
        if ve_src[par]:
            sy.wait_ge(kb.s_ve.h, ve_src[par])
        sdl[par].inc(sy.dma_start(
            out=sb_, in_=accd[ob * 128:(ob + 1) * 128,
                              tcx * TC:(tcx + 1) * TC]), 16)
        ve_src[par] = kb.s_ve.n + 1
        return sb_, [(sdl[par], sdl[par].n)]

    emit_postln(kb, src_of, x_read, x_writes, last=last,
                y_of=lambda ob, tcx: ep_y1[:, ob * TC:(ob + 1) * TC],
                xin_buf=ep_xin1, aux=ep_aux1)



_BUILD_CACHE = {}

_EMITTERS = {}


def _get_emitters():
    if not _EMITTERS:
        _EMITTERS.update({"l0a": emit_l0_attn, "l0f": emit_l0_ffn,
                          "l1a": emit_l1_attn, "l1f": emit_l1_ffn})
    return _EMITTERS


def _build_launch(stages):
    key = tuple(stages)
    if key in _BUILD_CACHE:
        return _BUILD_CACHE[key]
    nc = bass.Bass("TRN2", target_bir_lowering=False)
    kb = KB(nc)
    em = _get_emitters()
    all_eng = [mybir.EngineType.PE, mybir.EngineType.DVE,
               mybir.EngineType.Activation, mybir.EngineType.Pool,
               mybir.EngineType.SP]
    with nc.Block() as block:
        with nc.body("prog", valid_engines=all_eng):
            kb.load_consts(C_W if "l1a" in stages else C_TRILA[0])
            x_in = kb.inp("x_in", [HID, TL], F32)
            x_out = kb.outp("x_out", [HID, TL], F32)
            emit_load_xb(kb, x_in)
            n = len(stages)
            for i, st in enumerate(stages):
                is_last = (i == n - 1)
                xr = x_in if i == 0 else kb.scratch("xpark", [HID, TL], F32)
                xw = [x_out] if is_last else [
                    kb.scratch("xpark", [HID, TL], F32)]
                em[st](kb, xr, xw, last=is_last)
            kb.sy.wait_ge(kb.s_st.h, kb.s_st.n)
    _BUILD_CACHE[key] = (nc, kb)
    return nc, kb


def _prep_weights(inputs):
    w = {}
    w["consts"] = _consts_arr()
    if "L0_attn_Wq" in inputs:
        w["l0a_wq"] = _lhsT_pack(np.ascontiguousarray(inputs["L0_attn_Wq"]))
        w["l0a_wk"] = _lhsT_pack(np.ascontiguousarray(inputs["L0_attn_Wk"]))
        w["l0a_wg"] = _lhsT_pack(np.ascontiguousarray(inputs["L0_attn_Wg"]))
        wv = inputs["L0_attn_Wv"]
        w["l0a_wv"] = np.ascontiguousarray(
            wv.reshape(16, 128, 8, 512).transpose(2, 1, 0, 3).reshape(
                8, 128, 16 * 512))
        wo_p = _lhsT_pack(np.ascontiguousarray(inputs["L0_attn_Wo"]))
        perm = [r * 8 + h * 4 + j
                for h in range(2) for r in range(4) for j in range(4)]
        w["l0a_wo"] = np.ascontiguousarray(
            wo_p.reshape(16, 128, 32, 128)[:, :, perm, :].reshape(
                16, 128, 4096))
        wgt = inputs["L0_attn_Wgate"]
        w["l0a_wgate"] = np.ascontiguousarray(
            wgt.reshape(2, 128, 4).transpose(1, 0, 2).reshape(128, 8))
    if "L1_attn_Wq" in inputs:
        w["l1a_wq"] = _lhsT_pack(np.ascontiguousarray(inputs["L1_attn_Wq"]))
        w["l1a_wk"] = _lhsT_pack(np.ascontiguousarray(inputs["L1_attn_Wk"]))
        w["l1a_ws"] = _lhsT_pack(np.ascontiguousarray(inputs["L1_attn_Ws"]))
        w["l1a_wg"] = _lhsT_pack(np.ascontiguousarray(inputs["L1_attn_Wg"]))
        w["l1a_wo"] = _lhsT_pack(np.ascontiguousarray(inputs["L1_attn_Wo"]))
        wv1 = inputs["L1_attn_Wv"]
        w["l1a_wv"] = np.ascontiguousarray(
            wv1.reshape(16, 128, 4, 512).transpose(2, 1, 0, 3).reshape(
                4, 128, 16 * 512))
        w["l1a_gnw"] = np.ascontiguousarray(
            inputs["L1_attn_g_norm_weight"].reshape(1, 128).astype(
                np.float32))
    if "L1_ffn_gate_w" in inputs:
        gwT = np.ascontiguousarray(inputs["L1_ffn_gate_w"].T)
        w["l1f_gw"] = np.ascontiguousarray(
            gwT.reshape(16, 128, 8).transpose(1, 0, 2).reshape(128, 128))
        w["l1f_eb"] = np.ascontiguousarray(
            inputs["L1_ffn_expert_bias"].reshape(1, 8).astype(np.float32))
        w["l1f_w1"] = np.stack([_lhsT_pack(np.ascontiguousarray(
            inputs["L1_ffn_w1"][e].T)) for e in range(8)])
        w["l1f_w3"] = np.stack([_lhsT_pack(np.ascontiguousarray(
            inputs["L1_ffn_w3"][e].T)) for e in range(8)])
        w["l1f_w2"] = np.stack([_lhsT_pack(np.ascontiguousarray(
            inputs["L1_ffn_w2"][e].T)) for e in range(8)])
        w["l1f_sw1"] = _lhsT_pack(np.ascontiguousarray(inputs["L1_ffn_sw1"].T))
        w["l1f_sw3"] = _lhsT_pack(np.ascontiguousarray(inputs["L1_ffn_sw3"].T))
        w["l1f_sw2"] = _lhsT_pack(np.ascontiguousarray(inputs["L1_ffn_sw2"].T))
    if "L0_ffn_w_gate" in inputs:
        w["l0f_wg"] = _lhsT_pack(
            np.ascontiguousarray(inputs["L0_ffn_w_gate"].T))
        w["l0f_wu"] = _lhsT_pack(np.ascontiguousarray(inputs["L0_ffn_w_up"].T))
        wdT = np.ascontiguousarray(inputs["L0_ffn_w_down"].T)
        wdp = _lhsT_pack(wdT)
        w["l0f_wd"] = np.ascontiguousarray(
            wdp.reshape(4, 4, 128, 8192).transpose(0, 2, 1, 3).reshape(
                4, 128, 4 * 8192))
    return w


def run_chain(x, w, launch_groups, percore_extra=None):
    xs = []
    for c in range(NCORES):
        b, q = c // 4, c % 4
        xs.append(np.ascontiguousarray(
            x[b, q * TL:(q + 1) * TL, :].T.astype(np.float32)))

    for stages in launch_groups:
        nc, kb = _build_launch(stages)
        in_maps = []
        for c in range(NCORES):
            m = {}
            for name in kb.inputs:
                if name == "x_in":
                    m[name] = xs[c]
                elif percore_extra and name in percore_extra:
                    m[name] = percore_extra[name][c]
                elif name in w:
                    m[name] = w[name]
                else:
                    raise KeyError(f"missing input {name}")
            in_maps.append(m)
        res = run_bass_kernel_spmd(nc, in_maps, core_ids=list(range(NCORES)))
        xs = [np.asarray(res.results[c]["x_out"]) for c in range(NCORES)]

    out = np.empty((B, T, HID), np.float32)
    for c in range(NCORES):
        b, q = c // 4, c % 4
        out[b, q * TL:(q + 1) * TL, :] = xs[c].T
    return out


def kernel(**inputs):
    inputs = {k: np.asarray(v) for k, v in inputs.items()}
    x = inputs["L0_attn_hidden_states"]
    w = _prep_weights(inputs)
    pe = _percore_weights(inputs)
    return run_chain(x, w, LAUNCH_GROUPS, pe)


def _percore_weights(inputs):
    pe = {}
    if "L0_attn_Wq_exp" in inputs:
        def xpack(wexp, c):
            outs = []
            for h in range(2):
                gh = 2 * (c % 4) + h
                A = wexp[gh].reshape(2, 128, 6, 2, 128)
                outs.append(A.transpose(1, 2, 3, 0, 4).reshape(128, 3072))
            return np.ascontiguousarray(np.concatenate(outs, axis=1))
        pe["l0a_qexp"] = [xpack(inputs["L0_attn_Wq_exp"], c) for c in range(8)]
        pe["l0a_kexp"] = [xpack(inputs["L0_attn_Wk_exp"], c) for c in range(8)]
    return pe


# version 07
# speedup vs baseline: 1.1142x; 1.1142x over previous
import numpy as np
import concourse.bass as bass
import concourse.mybir as mybir
from concourse.bass_utils import run_bass_kernel_spmd

F32 = mybir.dt.float32
BF16 = mybir.dt.bfloat16
AF = mybir.ActivationFunctionType
ALU = mybir.AluOpType

NCORES = 8
ARENA_COLS = 80 * 1024
B = 2
T = 4096
TL = 1024
HID = 2048
NB = HID // 128
TC = 512
LN_EPS = 1e-5
GROUPS = [[0, 1, 2, 3], [4, 5, 6, 7]]

LAUNCH_GROUPS = [["l0a", "l0f", "l1a", "l1f"]]



def _lhsT_pack(w):
    din, dout = w.shape
    kb, ob = din // 128, dout // 128
    a = w.reshape(kb, 128, ob, 128).transpose(2, 1, 0, 3).reshape(ob, 128, din)
    return np.ascontiguousarray(a)


C_IDENT = (0, 128)
C_ONES = (128, 256)
C_IOTA4 = (256, 260)
C_IOTA8 = (260, 268)
C_IOTA6 = (268, 274)
C_IOTA4M = (274, 278)
C_IOTA8M = (278, 286)
C_M70 = (286, 287)
C_TRILA = (287, 543)
C_W = 543


def _consts_arr():
    ident = np.eye(128, dtype=np.float32)
    ones = np.ones((128, 128), np.float32)
    iota4 = np.tile(np.arange(4, dtype=np.float32)[None, :], (128, 1))
    iota8 = np.tile(np.arange(8, dtype=np.float32)[None, :], (128, 1))
    iota6 = np.tile(np.arange(6, dtype=np.float32)[None, :], (128, 1))
    j = np.arange(128)[:, None]
    tt = np.arange(256)[None, :]
    trilA = (j <= tt).astype(np.float32)
    return np.ascontiguousarray(np.concatenate(
        [ident, ones, iota4, iota8, iota6, iota4 - 4.0, iota8 - 8.0,
         np.full((128, 1), -70.0, np.float32), trilA], axis=1))



class Sem:
    __slots__ = ("h", "n")

    def __init__(self, nc, name):
        self.h = nc.alloc_semaphore(name)
        self.n = 0

    def inc(self, instr, k=1):
        instr.then_inc(self.h, k)
        self.n += k
        return self.n

    def wait(self, eng, n=None):
        eng.wait_ge(self.h, self.n if n is None else n)


class KB:
    def __init__(self, nc):
        self.nc = nc
        self.t = nc.tensor
        self.v = nc.vector
        self.s = nc.scalar
        self.g = nc.gpsimd
        self.sy = nc.sync
        self.inputs = {}
        self.outputs = {}
        self.dram = {}
        self.sb = {}
        self.s_w = Sem(nc, "s_w")
        self.s_y = Sem(nc, "s_y")
        self.s_z = Sem(nc, "s_z")
        self.s_mm = Sem(nc, "s_mm")
        self.s_ve = Sem(nc, "s_ve")
        self.s_ac = Sem(nc, "s_ac")
        self.s_go = Sem(nc, "s_go")
        self.s_st = Sem(nc, "s_st")
        self.ncc = 0
        self._sems = {}
        self.AR = nc.alloc_sbuf_tensor("arena", [128, ARENA_COLS], BF16)
        self.ARF = self.AR[:, :].bitcast(F32)
        self.ps = [nc.alloc_psum_tensor(f"ps{i}", [128, 512], F32)
                   for i in range(8)]
        self.all_sems = [self.s_w, self.s_y, self.s_z, self.s_mm, self.s_ve,
                         self.s_ac, self.s_go]

    def sem(self, name):
        if name not in self._sems:
            self._sems[name] = Sem(self.nc, name)
        return self._sems[name]

    def engines(self):
        return [self.t, self.v, self.s, self.g, self.sy]

    @staticmethod
    def dma(eng, **kw):
        try:
            return eng.dma_start(**kw)
        except ValueError:
            return eng.dma_start(**kw)

    def vsync(self):
        self.v.wait_ge(self.s_ve.h, self.s_ve.n)

    def ssync(self):
        self.s.wait_ge(self.s_ac.h, self.s_ac.n)

    def barrier(self):
        for eng in self.engines():
            for sem in self.all_sems + list(self._sems.values()):
                if sem.n:
                    eng.wait_ge(sem.h, sem.n)

    def cc_sem(self):
        s = Sem(self.nc, f"s_cc{self.ncc}")
        self.ncc += 1
        return s

    def inp(self, name, shape, dtype=F32):
        if name not in self.inputs:
            self.inputs[name] = self.nc.dram_tensor(
                name, list(shape), dtype, kind="ExternalInput").ap()
        return self.inputs[name]

    def outp(self, name, shape, dtype=F32):
        if name not in self.outputs:
            self.outputs[name] = self.nc.dram_tensor(
                name, list(shape), dtype, kind="ExternalOutput").ap()
        return self.outputs[name]

    def scratch(self, name, shape, dtype):
        if name not in self.dram:
            self.dram[name] = self.nc.dram_tensor(name, list(shape), dtype).ap()
        return self.dram[name]

    def sbuf(self, name, shape, dtype):
        if name not in self.sb:
            self.sb[name] = self.nc.alloc_sbuf_tensor(name, list(shape), dtype)
        return self.sb[name]

    def load_consts(self, width=C_W):
        cst = self.inp("consts", [128, C_W], F32)
        self.cf = self.sbuf("consts_f", [128, width], F32)
        self.cbf = self.sbuf("consts_b", [128, width], BF16)
        self.s_w.inc(self.g.dma_start(out=self.cf[:, :],
                                      in_=cst[:, 0:width]), 16)
        self.s_w.inc(self.g.dma_start(out=self.cbf[:, :],
                                      in_=cst[:, 0:width]), 16)

    def cs(self, which, dtype=BF16):
        t = self.cbf if dtype == BF16 else self.cf
        a, b = which
        return t[:, a:b]

    def ones_col(self, dtype=BF16):
        return self.cs(C_ONES, dtype)[:, 0:1]

    def ones_row(self, dtype=BF16):
        return self.cs(C_ONES, dtype)[0:1, :]


def nc_switch4(kb):
    g = kb.g
    if not hasattr(kb, "_sw_idx"):
        pid = g.partition_id()
        kb._sw_idx = g.compute_val(pid % 4)
    return kb.nc.Switch(engines=[g], index=[kb._sw_idx], n=4)


def emit_load_xb(kb, x_src_ap):
    xb = kb.sbuf("xb", [128, NB * TL], BF16)
    for i in range(NB):
        kb.s_w.inc(kb.g.dma_start(out=xb[:, i * TL:(i + 1) * TL],
                                  in_=x_src_ap[i * 128:(i + 1) * 128, :]), 16)
    return xb



def emit_postln(kb, src_of, x_read_ap, x_write_aps, last=False, y_of=None,
                xin_buf=None, aux=None):
    t, v, s, sy = kb.t, kb.v, kb.s, kb.sy
    xb = kb.sb["xb"]
    assert y_of is not None and xin_buf is not None and aux is not None
    y2, st_, mub, rsb = aux["y2"], aux["stat"], aux["mub"], aux["rsb"]

    p_s, p_q = kb.ps[4], kb.ps[5]
    p_b0, p_b1 = kb.ps[6], kb.ps[7]

    sx = [kb.sem("ep_x0"), kb.sem("ep_x1")]
    sy.wait_ge(kb.s_y.h, kb.s_y.n)
    ve_xin = [0, 0]
    mm_y2 = [0, 0]
    wsem = kb.s_st if last else kb.s_y
    for tcx in range(2):
        tok = slice(tcx * TC, (tcx + 1) * TC)
        v.wait_ge(kb.s_ac.h, kb.s_ac.n)
        if wsem.n:
            v.wait_ge(wsem.h, wsem.n)
        for ob in range(NB):
            par = ob % 2
            xi = xin_buf[:, par * TC:(par + 1) * TC]
            if ve_xin[par]:
                sy.wait_ge(kb.s_ve.h, ve_xin[par])
            sx[par].inc(sy.dma_start(
                out=xi, in_=x_read_ap[ob * 128:(ob + 1) * 128, tok]), 16)
            src, waits = src_of(ob, tcx)
            sx[par].wait(v)
            for (sm, cnt) in waits:
                v.wait_ge(sm.h, cnt)
            yo = y_of(ob, tcx)
            kb.s_ve.inc(v.tensor_add(yo, src, xi))
            ve_xin[par] = kb.s_ve.n
            y_done = kb.s_ve.n
            y2i = y2[:, par * TC:(par + 1) * TC]
            s.wait_ge(kb.s_ve.h, y_done)
            if mm_y2[par]:
                s.wait_ge(kb.s_mm.h, mm_y2[par])
            kb.s_ac.inc(s.activation(y2i, yo, AF.Square))
            t.wait_ge(kb.s_ac.h, kb.s_ac.n)
            t.wait_ge(kb.s_ve.h, y_done)
            i1 = t.matmul(p_s[0:1, :], kb.ones_col(F32), yo,
                          start=(ob == 0), stop=(ob == NB - 1))
            i2 = t.matmul(p_q[0:1, :], kb.ones_col(BF16), y2i,
                          start=(ob == 0), stop=(ob == NB - 1))
            kb.s_mm.inc(i2)
            mm_y2[par] = kb.s_mm.n
            if ob == NB - 1:
                kb.s_mm.inc(i1)
        s0 = st_[0:1, 0:TC]
        s1 = st_[0:1, TC:2 * TC]
        s2 = st_[0:1, 2 * TC:3 * TC]
        mu, rstd, t2 = s0, s1, s2
        v.wait_ge(kb.s_mm.h, kb.s_mm.n)
        kb.s_ve.inc(v.tensor_copy(s0, p_s[0:1, :]))
        kb.s_ve.inc(v.tensor_copy(s1, p_q[0:1, :]))
        kb.vsync()
        kb.s_ve.inc(v.tensor_scalar(s0, s0, 1.0 / HID, None, ALU.mult))
        kb.vsync()
        kb.s_ve.inc(v.tensor_tensor(s2, s0, s0, ALU.mult))
        kb.vsync()
        kb.s_ve.inc(v.scalar_tensor_tensor(s1, s1, 1.0 / HID, s2,
                                           ALU.mult, ALU.subtract))
        kb.vsync()
        kb.s_ve.inc(v.tensor_scalar(s1, s1, LN_EPS, None, ALU.add))
        s.wait_ge(kb.s_ve.h, kb.s_ve.n)
        kb.s_ac.inc(s.activation(s1, s1, AF.Sqrt))
        v.wait_ge(kb.s_ac.h, kb.s_ac.n)
        kb.s_ve.inc(v.reciprocal(s1, s1))
        kb.vsync()
        kb.s_ve.inc(v.tensor_tensor(s2, s0, s1, ALU.mult))
        t.wait_ge(kb.s_ve.h, kb.s_ve.n)
        kb.s_mm.inc(t.matmul(p_b0[:, :], kb.ones_row(F32), rstd,
                             start=True, stop=True))
        kb.s_mm.inc(t.matmul(p_b1[:, :], kb.ones_row(F32), t2,
                             start=True, stop=True))
        v.wait_ge(kb.s_mm.h, kb.s_mm.n)
        kb.s_ve.inc(v.tensor_copy(rsb[:, :], p_b0[:, :]))
        kb.s_ve.inc(v.tensor_copy(mub[:, :], p_b1[:, :]))
        bc = kb.s_ve.n
        for ob in range(NB):
            yo = y_of(ob, tcx)
            v.wait_ge(kb.s_ve.h, bc)
            kb.s_ve.inc(v.tensor_tensor(yo, yo, rsb[:, :], ALU.mult))
            kb.vsync()
            kb.s_ve.inc(v.tensor_tensor(yo, yo, mub[:, :], ALU.subtract))
            nrm = kb.s_ve.n
            s.wait_ge(kb.s_ve.h, nrm)
            kb.s_ac.inc(s.activation(
                xb[:, ob * TL + tcx * TC: ob * TL + tcx * TC + TC],
                yo, AF.Copy))
            sy.wait_ge(kb.s_ve.h, nrm)
            for ap in x_write_aps:
                wsem.inc(sy.dma_start(out=ap[ob * 128:(ob + 1) * 128, tok],
                                      in_=yo), 16)



def emit_l0_ffn(kb, x_read, x_writes, last=False):
    t, v, s, g, sy = kb.t, kb.v, kb.s, kb.g, kb.sy
    xb = kb.sb["xb"]
    HB = 64
    wg = kb.inp("l0f_wg", [HB, 128, HID])
    wu = kb.inp("l0f_wu", [HB, 128, HID])
    wd = kb.inp("l0f_wd", [4, 128, 4 * 8192])
    hpark = kb.scratch("l0f_hpark", [8192, TL], BF16)

    AR, ARF = kb.AR, kb.ARF
    wg_sb = AR[:, 0:4096]
    wu_sb = AR[:, 4096:8192]
    h_sb = AR[:, 8192:10240]
    sg_sb = AR[:, 10240:12288]
    ep_y = ARF[:, 23552:31744]
    ep_xin = ARF[:, 31744:32768]
    ep_aux = {"y2": AR[:, 46080:47104], "stat": ARF[0:1, 32768:34304],
              "mub": ARF[:, 34304:34816], "rsb": ARF[:, 34816:35328]}

    kb.barrier()

    swp = [kb.sem("l0f_w0"), kb.sem("l0f_w1")]
    shw = [kb.sem("l0f_hw0"), kb.sem("l0f_hw1")]
    mm_at = [0, 0]
    ac_at = [0, 0]
    ve_at = [0, 0]
    dma_at = [0, 0]
    for hb in range(HB):
        par = hb % 2
        wgs = wg_sb[:, par * HID:(par + 1) * HID]
        wus = wu_sb[:, par * HID:(par + 1) * HID]
        if mm_at[par]:
            g.wait_ge(kb.s_mm.h, mm_at[par])
        swp[par].inc(g.dma_start(out=wgs, in_=wg[hb]), 16)
        swp[par].inc(g.dma_start(out=wus, in_=wu[hb]), 16)
        pg = (kb.ps[4 * par + 0], kb.ps[4 * par + 1])
        pu = (kb.ps[4 * par + 2], kb.ps[4 * par + 3])
        swp[par].wait(t)
        if ac_at[par]:
            t.wait_ge(kb.s_ac.h, ac_at[par])
            t.wait_ge(kb.s_ve.h, ve_at[par])
        for half in range(2):
            for k in range(NB):
                rhs = xb[:, k * TL + half * 512: k * TL + half * 512 + 512]
                st, sp = (k == 0), (k == NB - 1)
                i1 = t.matmul(pg[half][:, :],
                              wgs[:, k * 128:(k + 1) * 128], rhs,
                              start=st, stop=sp)
                i2 = t.matmul(pu[half][:, :],
                              wus[:, k * 128:(k + 1) * 128], rhs,
                              start=st, stop=sp)
                if sp:
                    kb.s_mm.inc(i1)
                    kb.s_mm.inc(i2)
        mm_at[par] = kb.s_mm.n
        hs = h_sb[:, par * TL:(par + 1) * TL]
        sgs = sg_sb[:, par * TL:(par + 1) * TL]
        s.wait_ge(kb.s_mm.h, mm_at[par])
        for half in range(2):
            kb.s_ac.inc(s.activation(sgs[:, half * 512:half * 512 + 512],
                                     pg[half][:, :], AF.Silu))
        ac_at[par] = kb.s_ac.n
        v.wait_ge(kb.s_ac.h, ac_at[par])
        if dma_at[par]:
            v.wait_ge(shw[par].h, dma_at[par])
        for half in range(2):
            kb.s_ve.inc(v.tensor_tensor(hs[:, half * 512:half * 512 + 512],
                                        pu[half][:, :],
                                        sgs[:, half * 512:half * 512 + 512],
                                        ALU.mult))
        ve_at[par] = kb.s_ve.n
        sy.wait_ge(kb.s_ve.h, ve_at[par])
        shw[par].inc(sy.dma_start(out=hpark[hb * 128:(hb + 1) * 128, :],
                                  in_=hs), 16)
        dma_at[par] = shw[par].n

    t.wait_ge(kb.s_ac.h, kb.s_ac.n)
    t.wait_ge(kb.s_ve.h, kb.s_ve.n)

    wd_sb = AR[:, 12288:45056]
    hh_sb = AR[:, 45056:46080]
    swd = kb.sem("l0f_wd")
    shh = [kb.sem("l0f_h0"), kb.sem("l0f_h1")]
    state = {"mm_down": 0, "mm_chunk": [0, 0]}

    def src_of(ob, tcx):
        og, j = ob // 4, ob % 4
        if j == 0:
            g.wait_ge(kb.s_ve.h, kb.s_ve.n)
            swd.inc(g.dma_start(out=wd_sb[:, :], in_=wd[og]), 16)
            swd.wait(t)
            t.wait_ge(kb.s_ve.h, kb.s_ve.n)
            sy.wait_ge(shw[0].h, shw[0].n)
            sy.wait_ge(shw[1].h, shw[1].n)
            for k in range(HB):
                par = k % 2
                hh = hh_sb[:, par * 512:(par + 1) * 512]
                if state["mm_chunk"][par]:
                    sy.wait_ge(kb.s_mm.h, state["mm_chunk"][par])
                shh[par].inc(sy.dma_start(
                    out=hh,
                    in_=hpark[k * 128:(k + 1) * 128,
                              tcx * TC:(tcx + 1) * TC]), 16)
                shh[par].wait(t)
                for jj in range(4):
                    ii = t.matmul(kb.ps[jj][:, :],
                                  wd_sb[:, jj * 8192 + k * 128:
                                        jj * 8192 + k * 128 + 128],
                                  hh, start=(k == 0), stop=(k == HB - 1))
                kb.s_mm.inc(ii)
                state["mm_chunk"][par] = kb.s_mm.n
            state["mm_down"] = kb.s_mm.n
        return kb.ps[j][:, :], [(kb.s_mm, state["mm_down"])]

    emit_postln(kb, src_of, x_read, x_writes, last=last,
                y_of=lambda ob, tcx: ep_y[:, ob * TC:(ob + 1) * TC],
                xin_buf=ep_xin, aux=ep_aux)


def emit_l0_attn(kb, x_read, x_writes, last=False):
    t, v, s, g, sy = kb.t, kb.v, kb.s, kb.g, kb.sy
    xb = kb.sb["xb"]
    H2, R = 2, 6
    wq = kb.inp("l0a_wq", [NB, 128, HID])
    wk = kb.inp("l0a_wk", [NB, 128, HID])
    wgp = kb.inp("l0a_wg", [32, 128, HID])
    wvp = kb.inp("l0a_wv", [8, 128, NB * 512])
    wop = kb.inp("l0a_wo", [NB, 128, 4096])
    qexp = kb.inp("l0a_qexp", [128, H2 * R * 2 * 256])
    kexp = kb.inp("l0a_kexp", [128, H2 * R * 2 * 256])
    wgate = kb.inp("l0a_wgate", [128, 8])

    qk_in = kb.scratch("l0a_qkstage", [2 * HID, TL], BF16)
    qk_out = kb.scratch("l0a_qkgat", [8 * HID, TL], BF16)
    v_in = kb.scratch("l0a_vstage", [TL, 4096], BF16)
    v_out = kb.scratch("l0a_vgat", [T, 4096], BF16)
    o_in = kb.scratch("l0a_ostage", [TL, T], BF16)
    o_oH = [kb.scratch(f"l0a_og{i}", [2048, T], BF16) for i in range(2)]
    oacc = kb.scratch("l0a_oacc", [TL, T], F32)
    gpark = kb.scratch("l0a_gpark", [4096, TL], BF16)

    AR, ARF = kb.AR, kb.ARF
    v_sb = AR[:, 0:16384]
    ke_sb = AR[:, 16384:24576]
    mb_sb = AR[:, 24576:28672]
    qxp_sb = AR[:, 28672:31744]
    kxp_sb = AR[:, 31744:34816]
    rwr = AR[0:1, 34816:38912]
    qe_sb = AR[:, 38912:39936]
    wnb = AR[:, 39936:40448]
    wn_row = AR[0:1, 40448:40960]
    q_sb = AR[:, 40960:49152]
    k_sb = AR[:, 49152:57344]
    rstg = AR[0:6, 57344:57472]
    wst_a = AR[:, 0:4096]
    wvs_a = AR[:, 4096:20480]
    st_b = AR[:, 20480:24576]
    o_loc = AR[:, 0:32768]
    sg_f = AR[:, 32768:34816]
    wo_f = AR[:, 34816:43008]
    g_f = AR[:, 43008:45056]
    y2_l0a = AR[:, 45056:46080]
    stf = ARF[:, 28736:32832]
    Ssh = ARF[:, 32832:33856]
    dn_row = ARF[0:1, 33856:34368]
    rw_sb = ARF[:, 34368:34560]
    sm_sb = ARF[:, 34560:34752]
    lg_sb = ARF[:, 34752:34768]
    t4 = ARF[:, 34768:34800]
    ep_y_l0a = ARF[:, 28736:36928]
    ep_xin_l0a = ARF[:, 36928:37952]
    mub_l0a = ARF[:, 37952:38464]
    rsb_l0a = ARF[:, 38464:38976]
    stat_l0a = ARF[0:1, 38976:40512]
    rwsd = kb.scratch("l0a_rwsd", [6, 4096], BF16)

    P_sb = xb[:, :]

    kb.barrier()
    swp = [kb.sem("l0a_w0"), kb.sem("l0a_w1")]
    sst = [kb.sem("l0a_s0"), kb.sem("l0a_s1")]
    sqt = [kb.sem("l0a_q0"), kb.sem("l0a_q1")]
    skt = [kb.sem("l0a_k0"), kb.sem("l0a_k1")]
    svr = kb.sem("l0a_vrd")
    sxp = kb.sem("l0a_xp")
    soa = [kb.sem("l0a_oa0"), kb.sem("l0a_oa1")]
    sob = kb.sem("l0a_ob")
    sgf = [kb.sem("l0a_g0"), kb.sem("l0a_g1")]
    swo = [kb.sem("l0a_wo0"), kb.sem("l0a_wo1")]
    sol = kb.sem("l0a_ol")

    pid4 = None

    mm_at = [0, 0]
    ve_at = [0, 0]
    st_at = [0, 0]

    def proj_fm(wpack, n_ob, dst, dst_row0):
        for ob in range(n_ob):
            par = ob % 2
            ws = wst_a[:, par * HID:(par + 1) * HID]
            if mm_at[par]:
                g.wait_ge(kb.s_mm.h, mm_at[par])
            swp[par].inc(g.dma_start(out=ws, in_=wpack[ob]), 16)
            swp[par].wait(t)
            pq = (kb.ps[4 * par + 0], kb.ps[4 * par + 1])
            if ve_at[par]:
                t.wait_ge(kb.s_ve.h, ve_at[par])
            for half in range(2):
                for k in range(NB):
                    ii = t.matmul(pq[half][:, :],
                                  ws[:, k * 128:(k + 1) * 128],
                                  xb[:, k * TL + half * 512:
                                     k * TL + half * 512 + 512],
                                  start=(k == 0), stop=(k == NB - 1))
                kb.s_mm.inc(ii)
            mm_at[par] = kb.s_mm.n
            stg = st_b[:, par * TL:(par + 1) * TL]
            v.wait_ge(kb.s_mm.h, mm_at[par])
            if st_at[par]:
                v.wait_ge(sst[par].h, st_at[par])
            for half in range(2):
                kb.s_ve.inc(v.tensor_copy(stg[:, half * 512:half * 512 + 512],
                                          pq[half][:, :]))
            ve_at[par] = kb.s_ve.n
            sy.wait_ge(kb.s_ve.h, ve_at[par])
            sst[par].inc(sy.dma_start(
                out=dst[dst_row0 + ob * 128: dst_row0 + (ob + 1) * 128, :],
                in_=stg), 16)
            st_at[par] = sst[par].n

    proj_fm(wq, NB, qk_in, 0)
    proj_fm(wk, NB, qk_in, HID)

    g.wait_ge(sst[0].h, sst[0].n)
    g.wait_ge(sst[1].h, sst[1].n)
    cc1 = kb.cc_sem()
    cc1.inc(g.collective_compute(
        "AllGather", ALU.bypass,
        replica_groups=GROUPS,
        ins=[qk_in[:, :]],
        outs=[qk_out[:, :]],
    ), 1)

    t.wait_ge(kb.s_ve.h, kb.s_ve.n)
    vw_at = list(mm_at)
    vv_at = [0, 0]
    for ch in range(8):
        par = ch % 2
        wvs = wvs_a[:, par * NB * 512:(par + 1) * NB * 512]
        if vw_at[par]:
            g.wait_ge(kb.s_mm.h, vw_at[par])
        swp[par].inc(g.dma_start(out=wvs, in_=wvp[ch]), 16)
        swp[par].wait(t)
        if ch > 0:
            t.wait_ge(kb.s_ve.h, vv_at[1 - par])
        for k in range(NB):
            for tb in range(8):
                ii = t.matmul(kb.ps[tb][:, :],
                              xb[:, k * TL + tb * 128:
                                 k * TL + tb * 128 + 128],
                              wvs[:, k * 512:(k + 1) * 512],
                              start=(k == 0), stop=(k == NB - 1))
            kb.s_mm.inc(ii)
        vw_at[par] = kb.s_mm.n
        v.wait_ge(kb.s_mm.h, vw_at[par])
        v.wait_ge(sst[0].h, sst[0].n)
        v.wait_ge(sst[1].h, sst[1].n)
        for tb in range(8):
            kb.s_ve.inc(v.tensor_copy(st_b[:, tb * 512:(tb + 1) * 512],
                                      kb.ps[tb][:, :]))
        vv_at[par] = kb.s_ve.n
        sy.wait_ge(kb.s_ve.h, vv_at[par])
        sst[0].inc(sy.dma_start(
            out=v_in.rearrange("(tb p) c -> p tb c", p=128)[
                :, :, ch * 512:(ch + 1) * 512],
            in_=st_b.rearrange("p (tb c) -> p tb c", tb=8)), 16)

    g.wait_ge(sst[0].h, sst[0].n)
    cc1b = kb.cc_sem()
    cc1b.inc(g.collective_compute(
        "AllGather", ALU.bypass,
        replica_groups=GROUPS,
        ins=[v_in[:, :]],
        outs=[v_out[:, :]],
    ), 1)

    mm_at = list(vw_at)
    mv = max(vv_at)
    ve_at = [mv, mv]
    v.wait_ge(sst[0].h, sst[0].n)
    st_at = [0, 0]
    proj_fm(wgp, 32, gpark, 0)
    mm_gp = kb.s_mm.n
    sst_gp = (sst[0].n, sst[1].n)

    wga_sb = kb.sbuf("l0a_wga", [128, 8], BF16)
    g.wait_ge(kb.s_mm.h, kb.s_mm.n)
    sxp.inc(g.dma_start(out=wga_sb[:, :], in_=wgate), 16)
    svv = kb.sem("l0a_vvr")

    def qxp_tile(src, r, eb, db):
        base = (r * 2 + eb) * 256 + db * 128
        return src[:, base:base + 128]

    onesb = kb.ones_col(BF16)

    cc_oh = []
    for h in range(H2):
        if h == 0:
            g.wait_ge(cc1.h, 1)
        if h > 0:
            g.wait_ge(kb.s_mm.h, kb.s_mm.n)
        sxp.inc(g.dma_start(out=qxp_sb,
                            in_=qexp[:, h * 3072:(h + 1) * 3072]), 16)
        sxp.inc(g.dma_start(out=kxp_sb,
                            in_=kexp[:, h * 3072:(h + 1) * 3072]), 16)
        qk_r = qk_out.rearrange("(r4 rows) t -> rows r4 t", rows=4096)
        for i4 in nc_switch4(kb):
            for db in range(2):
                qrow_q = i4 * 512 + h * 256 + db * 128
                g.dma_start(
                    out=q_sb.rearrange("p (db r4 t) -> p db r4 t",
                                       db=2, r4=4)[:, db],
                    in_=qk_r[qrow_q:qrow_q + 128]).then_inc(svr.h, 16)
                g.dma_start(
                    out=k_sb.rearrange("p (db r4 t) -> p db r4 t",
                                       db=2, r4=4)[:, db],
                    in_=qk_r[2048 + qrow_q:2048 + qrow_q + 128]).then_inc(
                        svr.h, 16)
        svr.n += 4 * 16

        sxp.wait(t)
        svr.wait(t)
        t.wait_ge(kb.s_ve.h, kb.s_ve.n)
        t.wait_ge(kb.s_ac.h, kb.s_ac.n)
        if h == 0:
            v.wait_ge(sst[0].h, sst_gp[0])
            v.wait_ge(sst[1].h, sst_gp[1])
        lg_mm = []
        for tile in range(32):
            par = tile % 2
            pb = kb.ps[4 + par]
            if tile >= 2:
                t.wait_ge(kb.s_ac.h, kb.s_ac.n)
            for db in range(2):
                ii = t.matmul(pb[:, 0:4],
                              q_sb[:, db * 4096 + tile * 128:
                                   db * 4096 + tile * 128 + 128],
                              wga_sb[:, db * 4:(db + 1) * 4],
                              start=(db == 0), stop=(db == 1))
            kb.s_mm.inc(ii)
            lg_mm.append(kb.s_mm.n)
            s.wait_ge(kb.s_mm.h, kb.s_mm.n)
            m4 = t4[:, 0:1]
            sm4 = t4[:, 1:2]
            i1 = t4[:, 2:3]
            m1 = t4[:, 3:4]
            m2 = t4[:, 4:5]
            wsumr = t4[:, 5:6]
            i1p = t4[:, 6:7]
            sc4 = lg_sb[:, 0:4]
            eq4 = t4[:, 8:12]
            tt4 = t4[:, 12:16]
            oh6a = t4[:, 16:22]
            oh6b = t4[:, 22:28]
            v.wait_ge(kb.s_mm.h, kb.s_mm.n)
            kb.s_ve.inc(v.reduce_max(m4, pb[:, 0:4], mybir.AxisListType.X,
                                     negate=True))
            kb.vsync()
            s.wait_ge(kb.s_ve.h, kb.s_ve.n)
            kb.s_ac.inc(s.activation(sc4, pb[:, 0:4], AF.Exp, bias=m4,
                                     accum_out=sm4))
            v.wait_ge(kb.s_ac.h, kb.s_ac.n)
            kb.s_ve.inc(v.reciprocal(sm4, sm4))
            kb.vsync()
            kb.s_ve.inc(v.tensor_scalar(sc4, sc4, sm4, None, ALU.mult))
            kb.vsync()
            iom = kb.cs(C_IOTA4M, F32)
            kb.s_ve.inc(v.reduce_max(m1, sc4, mybir.AxisListType.X))
            kb.vsync()
            kb.s_ve.inc(v.tensor_single_scalar(eq4, sc4, m1, ALU.is_equal))
            kb.vsync()
            kb.s_ve.inc(v.tensor_mul(tt4, eq4, iom))
            kb.vsync()
            kb.s_ve.inc(v.tensor_reduce(i1, tt4, mybir.AxisListType.X,
                                        ALU.min))
            kb.vsync()
            kb.s_ve.inc(v.tensor_single_scalar(eq4, iom, i1, ALU.is_equal))
            kb.vsync()
            kb.s_ve.inc(v.scalar_tensor_tensor(tt4, eq4, -1e9, sc4,
                                               ALU.mult, ALU.add))
            kb.vsync()
            kb.s_ve.inc(v.reduce_max(m2, tt4, mybir.AxisListType.X))
            kb.vsync()
            kb.s_ve.inc(v.tensor_scalar(i1p, i1, 6.0, None, ALU.add))
            kb.vsync()
            io6 = kb.cs(C_IOTA6, F32)
            kb.s_ve.inc(v.tensor_single_scalar(oh6a, io6, i1p, ALU.is_equal))
            kb.s_ve.inc(v.tensor_single_scalar(eq4, tt4, m2, ALU.is_equal))
            kb.vsync()
            kb.s_ve.inc(v.tensor_mul(tt4, eq4, iom))
            kb.vsync()
            kb.s_ve.inc(v.tensor_reduce(i1, tt4, mybir.AxisListType.X,
                                        ALU.min))
            kb.vsync()
            kb.s_ve.inc(v.tensor_scalar(i1p, i1, 6.0, None, ALU.add))
            kb.vsync()
            kb.s_ve.inc(v.tensor_single_scalar(oh6b, io6, i1p, ALU.is_equal))
            kb.vsync()
            kb.s_ve.inc(v.tensor_add(wsumr, m1, m2))
            kb.vsync()
            kb.s_ve.inc(v.reciprocal(wsumr, wsumr))
            kb.vsync()
            kb.s_ve.inc(v.tensor_scalar(m1, m1, wsumr, 0.5, ALU.mult,
                                        ALU.mult))
            kb.s_ve.inc(v.tensor_scalar(m2, m2, wsumr, 0.5, ALU.mult,
                                        ALU.mult))
            kb.vsync()
            rwt = rw_sb[:, tile * 6:(tile + 1) * 6]
            kb.s_ve.inc(v.memset(rwt[:, 0:2], 0.25))
            kb.s_ve.inc(v.tensor_scalar(oh6a, oh6a, m1, None, ALU.mult))
            kb.s_ve.inc(v.tensor_scalar(oh6b, oh6b, m2, None, ALU.mult))
            kb.vsync()
            kb.s_ve.inc(v.tensor_add(rwt[:, 2:6], oh6a[:, 2:6],
                                     oh6b[:, 2:6]))
            kb.vsync()
        kb.s_ve.inc(v.tensor_scalar(sm_sb[:, :], rw_sb[:, :], 0.0, None,
                                    ALU.is_gt))
        idf = kb.cs(C_IDENT, F32)
        srws = kb.sem("l0a_rws")
        for tile in range(32):
            t.wait_ge(kb.s_ve.h, kb.s_ve.n)
            kb.s_mm.inc(t.transpose(kb.ps[6][0:6, 0:128],
                                    rw_sb[:, tile * 6:(tile + 1) * 6], idf))
            v.wait_ge(kb.s_mm.h, kb.s_mm.n)
            if srws.n:
                v.wait_ge(srws.h, srws.n)
            kb.s_ve.inc(v.tensor_copy(rstg[:, :], kb.ps[6][0:6, 0:128]))
            t.wait_ge(kb.s_ve.h, kb.s_ve.n)
            sy.wait_ge(kb.s_ve.h, kb.s_ve.n)
            srws.inc(sy.dma_start(
                out=rwsd[:, tile * 128:(tile + 1) * 128], in_=rstg[:, :]), 16)
        if h == 0:
            g.wait_ge(cc1b.h, 1)
            g.wait_ge(kb.s_mm.h, mm_gp)
        for i4 in nc_switch4(kb):
            colh = i4 * 1024 + h * 512
            g.dma_start(
                out=v_sb.rearrange("p (tb c) -> p tb c", c=512),
                in_=v_out.rearrange("(tb p) c -> p tb c", p=128)[
                    :, :, colh:colh + 512]).then_inc(svv.h, 16)
        svv.n += 16
        g.wait_ge(srws.h, srws.n)

        ve_keb = [0, 0]
        ve_msk = 0
        mm_qe = 0
        ve_wnb = 0
        soa_t = [0, 0]
        for r in range(R):
            g.wait_ge(kb.s_mm.h, kb.s_mm.n)
            g.wait_ge(kb.s_ve.h, kb.s_ve.n)
            sob.inc(g.dma_start(out=rwr, in_=rwsd[r:r + 1, :]), 16)
            sob.wait(t)
            mm_mb = []
            for chn in range(8):
                w6 = max(ve_msk, ve_qe_evac if mm_qe else 0)
                if w6:
                    t.wait_ge(kb.s_ve.h, w6)
                i = t.matmul(
                    kb.ps[6][:, :], kb.ones_row(BF16),
                    rwr[:, chn * 512:(chn + 1) * 512],
                    start=True, stop=True)
                kb.s_mm.inc(i)
                mm_mb.append(kb.s_mm.n)
                v.wait_ge(kb.s_mm.h, kb.s_mm.n)
                kb.s_ve.inc(v.tensor_scalar(
                    mb_sb[:, chn * 512:(chn + 1) * 512], kb.ps[6][:, :],
                    0.0, None, ALU.is_gt))
                ve_msk = kb.s_ve.n
            for chn in range(8):
                par = chn % 2
                for eb in range(2):
                    pp = kb.ps[4 + par]
                    if ve_keb[par]:
                        t.wait_ge(kb.s_ve.h, ve_keb[par])
                    for db in range(2):
                        ii = t.matmul(pp[:, :],
                                      qxp_tile(kxp_sb, r, eb, db),
                                      k_sb[:, db * 4096 + chn * 512:
                                           db * 4096 + chn * 512 + 512],
                                      start=(db == 0), stop=(db == 1))
                    kb.s_mm.inc(ii)
                    v.wait_ge(kb.s_mm.h, kb.s_mm.n)
                    kb.s_ve.inc(v.tensor_tensor(
                        ke_sb[:, eb * 4096 + chn * 512:
                              eb * 4096 + chn * 512 + 512],
                        pp[:, :], mb_sb[:, chn * 512:(chn + 1) * 512],
                        ALU.mult))
                    ve_keb[par] = kb.s_ve.n
            ke_done = kb.s_ve.n
            if r == 0:
                svv.wait(t)
            for st_i in range(8):
                w6 = max(ve_msk, ve_qe_evac if mm_qe else 0, ve_wnb)
                if w6:
                    t.wait_ge(kb.s_ve.h, w6)
                for eb in range(2):
                    pq = kb.ps[6 + eb]
                    for db in range(2):
                        ii = t.matmul(pq[:, :],
                                      qxp_tile(qxp_sb, r, eb, db),
                                      q_sb[:, db * 4096 + st_i * 512:
                                           db * 4096 + st_i * 512 + 512],
                                      start=(db == 0), stop=(db == 1))
                    kb.s_mm.inc(ii)
                    mm_qe = kb.s_mm.n
                    v.wait_ge(kb.s_mm.h, mm_qe)
                    kb.s_ve.inc(v.tensor_scalar(
                        qe_sb[:, eb * 512:(eb + 1) * 512], pq[:, :],
                        0.0625, None, ALU.mult))
                    ve_qe_evac = kb.s_ve.n
                t.wait_ge(kb.s_ve.h, ve_qe_evac)
                ac_P = [0, 0]
                ac_of = {}
                ve_P2 = {}
                mm_S = [0, 0]
                for i in range(35):
                    if i < 32:
                        tk = i
                        par = tk % 2
                        pS = kb.ps[4 + par]
                        if ac_P[par]:
                            t.wait_ge(kb.s_ac.h, ac_P[par])
                        if tk == 0 and ke_done:
                            t.wait_ge(kb.s_ve.h, ke_done)
                        for eb in range(2):
                            ii = t.matmul(pS[:, :],
                                          ke_sb[:, eb * 4096 + tk * 128:
                                                eb * 4096 + tk * 128 + 128],
                                          qe_sb[:, eb * 512:(eb + 1) * 512],
                                          start=(eb == 0), stop=(eb == 1))
                        kb.s_mm.inc(ii)
                        mm_S[par] = kb.s_mm.n
                        s.wait_ge(kb.s_mm.h, mm_S[par])
                        kb.s_ac.inc(s.activation(
                            P_sb[:, tk * 512:(tk + 1) * 512], pS[:, :],
                            AF.Exp, bias=kb.cs(C_M70, F32)))
                        ac_P[par] = kb.s_ac.n
                        ac_of[tk] = kb.s_ac.n
                    td = i - 2
                    if 0 <= td < 32:
                        t.wait_ge(kb.s_ac.h, ac_of[td])
                        ii = t.matmul(kb.ps[7][0:1, :], onesb,
                                      P_sb[:, td * 512:(td + 1) * 512],
                                      start=(td == 0), stop=(td == 31))
                        kb.s_mm.inc(ii)
                        mm_dnl = kb.s_mm.n
                        if td == 31:
                            mm_dn = kb.s_mm.n
                        v.wait_ge(kb.s_mm.h, mm_dnl)
                        kb.s_ve.inc(v.tensor_scalar(
                            P_sb[:, td * 512:(td + 1) * 512],
                            P_sb[:, td * 512:(td + 1) * 512],
                            sm_sb[:, td * 6 + r: td * 6 + r + 1],
                            None, ALU.mult))
                        ve_P2[td] = kb.s_ve.n
                    ta = i - 3
                    if 0 <= ta < 32:
                        t.wait_ge(kb.s_ve.h, ve_P2[ta])
                        if ta == 0 and (soa_t[0] or soa_t[1]):
                            t.wait_ge(kb.s_ve.h, max(soa_t))
                        for dv in range(4):
                            ii = t.matmul(kb.ps[dv][:, :],
                                          v_sb[:, ta * 512 + dv * 128:
                                               ta * 512 + dv * 128 + 128],
                                          P_sb[:, ta * 512:(ta + 1) * 512],
                                          start=(ta == 0), stop=(ta == 31))
                        if ta == 31:
                            kb.s_mm.inc(ii)
                            mm_dn_av = kb.s_mm.n
                v.wait_ge(kb.s_mm.h, mm_dn)
                kb.s_ve.inc(v.reciprocal(dn_row, kb.ps[7][0:1, :]))
                kb.vsync()
                kb.s_ve.inc(v.tensor_tensor(
                    wn_row, rwr[:, st_i * 512:(st_i + 1) * 512], dn_row,
                    ALU.mult))
                t.wait_ge(kb.s_ve.h, kb.s_ve.n)
                kb.s_mm.inc(t.matmul(kb.ps[7][:, :], kb.ones_row(BF16),
                                     wn_row, start=True, stop=True))
                v.wait_ge(kb.s_mm.h, kb.s_mm.n)
                kb.s_ve.inc(v.tensor_copy(wnb, kb.ps[7][:, :]))
                ve_wnb = kb.s_ve.n
                kb.vsync()
                spar = st_i % 2
                stfp = stf[:, spar * 2048:(spar + 1) * 2048]
                v.wait_ge(kb.s_mm.h, mm_dn_av)
                if soa[spar].n:
                    v.wait_ge(soa[spar].h, soa[spar].n)
                for dv in range(4):
                    kb.s_ve.inc(v.tensor_tensor(
                        stfp[:, dv * 512:(dv + 1) * 512], kb.ps[dv][:, :],
                        wnb, ALU.mult))
                soa_t[spar] = kb.s_ve.n
                oev = kb.s_ve.n
                g.wait_ge(kb.s_ve.h, oev)
                if soa[spar].n:
                    g.wait_ge(soa[spar].h, soa[spar].n)
                soa[spar].inc(g.dma_start(
                    out=oacc.rearrange("(hh d p) c -> hh p d c",
                                       d=4, p=128)[h][
                        :, :, st_i * 512:(st_i + 1) * 512],
                    in_=stfp.rearrange("p (d c) -> p d c", d=4),
                    accum_op=(ALU.add if r > 0 else ALU.bypass)), 16)
        g.wait_ge(soa[0].h, soa[0].n)
        g.wait_ge(soa[1].h, soa[1].n)
        sob.inc(g.dma_start(out=o_in[h * 512:(h + 1) * 512, :],
                            in_=oacc[h * 512:(h + 1) * 512, :]), 16)
        g.wait_ge(sob.h, sob.n)
        cco = kb.cc_sem()
        cco.inc(g.collective_compute(
            "AllGather", ALU.bypass, replica_groups=GROUPS,
            ins=[o_in[h * 512:(h + 1) * 512, :]],
            outs=[o_oH[h][:, :]]), 1)
        cc_oh.append(cco)

    for cco in cc_oh:
        g.wait_ge(cco.h, 1)
    kb.s_go.inc(g.engine_nop())
    sy.wait_ge(kb.s_go.h, kb.s_go.n)
    sy.wait_ge(kb.s_mm.h, kb.s_mm.n)
    sy.wait_ge(kb.s_ve.h, kb.s_ve.n)
    for i4 in nc_switch4(kb):
        tok0 = i4 * TL
        for hh in range(2):
            g.dma_start(
                out=o_loc.rearrange("p (h2 rj t) -> p h2 rj t",
                                    h2=2, rj=16)[:, hh],
                in_=o_oH[hh].rearrange("(rj p) t -> p rj t", p=128)[
                    :, :, tok0:tok0 + TL]).then_inc(sol.h, 16)
    sol.n += 2 * 16
    gate_ve = []
    for pb in range(32):
        par = pb % 2
        gf = g_f[:, par * TL:(par + 1) * TL]
        if pb >= 2:
            sy.wait_ge(kb.s_ac.h, kb.s_ac.n)
        sgf[par].inc(sy.dma_start(
            out=gf, in_=gpark[pb * 128:(pb + 1) * 128, :]), 16)
        sgf[par].wait(s)
        sgg = sg_f[:, par * TL:(par + 1) * TL]
        if pb >= 2:
            s.wait_ge(kb.s_ve.h, gate_ve[pb - 2])
        kb.s_ac.inc(s.activation(sgg, gf, AF.Silu))
        v.wait_ge(kb.s_ac.h, kb.s_ac.n)
        if pb == 0:
            v.wait_ge(sol.h, sol.n)
        kb.s_ve.inc(v.tensor_tensor(o_loc[:, pb * TL:(pb + 1) * TL],
                                    o_loc[:, pb * TL:(pb + 1) * TL],
                                    sgg, ALU.mult))
        gate_ve.append(kb.s_ve.n)
    og_done = kb.s_ve.n

    state = {"mm": 0, "wo_at": [0, 0], "ve_bank": [0, 0, 0, 0]}

    def src_of(ob, tcx):
        par = ob % 2
        bank = ob % 4
        wos = wo_f[:, par * 4096:(par + 1) * 4096]
        if state["wo_at"][par]:
            g.wait_ge(kb.s_mm.h, state["wo_at"][par])
        swo[par].inc(g.dma_start(out=wos, in_=wop[ob]), 16)
        swo[par].wait(t)
        if ob == 0 and tcx == 0:
            t.wait_ge(kb.s_ve.h, og_done)
        if state["ve_bank"][bank]:
            t.wait_ge(kb.s_ve.h, state["ve_bank"][bank])
        for kk in range(32):
            ii = t.matmul(kb.ps[bank][:, :],
                          wos[:, kk * 128:(kk + 1) * 128],
                          o_loc[:, kk * TL + tcx * 512:
                                kk * TL + tcx * 512 + 512],
                          start=(kk == 0), stop=(kk == 31))
        kb.s_mm.inc(ii)
        state["wo_at"][par] = kb.s_mm.n
        state["ve_bank"][bank] = kb.s_ve.n + 1
        return kb.ps[bank][:, :], [(kb.s_mm, kb.s_mm.n)]

    emit_postln(kb, src_of, x_read, x_writes, last=last,
                y_of=lambda ob, tcx: ep_y_l0a[:, ob * 512:(ob + 1) * 512],
                xin_buf=ep_xin_l0a,
                aux={"y2": y2_l0a, "stat": stat_l0a,
                     "mub": mub_l0a, "rsb": rsb_l0a})



def emit_l1_attn(kb, x_read, x_writes, last=False):
    t, v, s, g, sy = kb.t, kb.v, kb.s, kb.g, kb.sy
    xb = kb.sb["xb"]
    CH = 256
    wqp = kb.inp("l1a_wq", [8, 128, HID])
    wkp = kb.inp("l1a_wk", [8, 128, HID])
    wsp = kb.inp("l1a_ws", [8, 128, HID])
    wvp = kb.inp("l1a_wv", [4, 128, NB * 512])
    wgp = kb.inp("l1a_wg", [NB, 128, HID])
    wop = kb.inp("l1a_wo", [NB, 128, HID])
    gnw = kb.inp("l1a_gnw", [1, 128])

    qk1_in = kb.scratch("l1a_qkstage", [2048, TL], BF16)
    qk1_out = kb.scratch("l1a_qkgat", [8192, TL], BF16)
    s1_in = kb.scratch("l1a_sstage", [TL, TL], F32)
    s1_out = kb.scratch("l1a_sgat", [4096, TL], F32)
    v1_in = kb.scratch("l1a_vstage", [TL, 2048], BF16)
    v1_out = kb.scratch("l1a_vgat", [4096, 2048], BF16)
    o1_in = kb.scratch("l1a_ostage", [512, T], BF16)
    o1_oS = [kb.scratch(f"l1a_og{i}", [512, T], BF16) for i in range(4)]
    g1park = kb.scratch("l1a_gpark", [2048, TL], BF16)

    AR, ARF = kb.AR, kb.ARF
    q12 = AR[:, 0:4096]
    k12 = AR[:, 4096:8192]
    v12 = AR[:, 8192:16384]
    ET_t = AR[:, 16384:20480]
    oT = AR[:, 20480:24576]
    s12 = ARF[:, 13664:17760]
    wst = AR[:, 0:4096]
    wvs = AR[:, 4096:12288]
    stb = AR[:, 12288:16384]
    sstg = ARF[:, 13664:15712]
    o_loc1 = AR[:, 0:16384]
    sg1 = AR[:, 16384:18432]
    gf1 = AR[:, 18432:20480]
    wo1 = AR[:, 20480:24576]
    Eb = AR[:, 24576:24704]
    ktm = AR[:, 24704:24832]
    A_sb = AR[:, 24832:25344]
    U_sb = AR[:, 25344:25856]
    Wf = AR[:, 25856:25984]
    WT_t = AR[:, 25984:26240]
    onb = AR[:, 26240:26368]
    gnwb = AR[:, 26368:26496]
    gnr = AR[0:1, 26496:26624]
    HKs_t = AR[:, 26624:26688]
    HVs_t = AR[:, 26688:26816]
    y2_l1a = AR[:, 26816:27840]
    q12b = AR[:, 61824:65920]
    k12b = AR[:, 65920:70016]
    v12b = AR[:, 70016:78208]
    Ef = ARF[:, 17760:17888]
    rz = ARF[:, 17888:18016]
    rz8 = ARF[:, 18016:18144]
    zc = ARF[0:1, 18144:18208]
    csum = ARF[0:1, 18208:18272]
    HKf_t = ARF[:, 18272:18336]
    HVf_t = ARF[:, 18336:18464]
    okf = ARF[:, 18464:18592]
    pvb = ARF[:, 18592:18720]
    ovf = ARF[:, 18720:18976]
    ov2 = ARF[:, 18976:19104]
    msq = ARF[:, 19104:19112]
    ep_y_l1a = ARF[:, 19112:27304]
    ep_xin_l1a = ARF[:, 27304:28328]
    mub_l1a = ARF[:, 28328:28840]
    rsb_l1a = ARF[:, 28840:29352]
    stat_l1a = ARF[0:1, 29352:30888]

    kb.barrier()
    swp = [kb.sem("l1a_w0"), kb.sem("l1a_w1")]
    sst = [kb.sem("l1a_s0"), kb.sem("l1a_s1")]
    svr = kb.sem("l1a_vrd")
    svr2 = kb.sem("l1a_vr2")
    sob = kb.sem("l1a_ob")
    sol = kb.sem("l1a_ol")
    sgf = [kb.sem("l1a_g0"), kb.sem("l1a_g1")]
    swo = [kb.sem("l1a_wo0"), kb.sem("l1a_wo1")]

    ps7b = kb.ps[7][:, :].bitcast(BF16)

    mm_at = [0, 0]
    ve_at = [0, 0]
    st_at = [0, 0]

    def proj_fm(wpack, n_ob, dst, dst_row0):
        for ob in range(n_ob):
            par = ob % 2
            ws = wst[:, par * HID:(par + 1) * HID]
            if mm_at[par]:
                g.wait_ge(kb.s_mm.h, mm_at[par])
            swp[par].inc(g.dma_start(out=ws, in_=wpack[ob]), 16)
            swp[par].wait(t)
            pq = (kb.ps[4 * par + 0], kb.ps[4 * par + 1])
            if ve_at[par]:
                t.wait_ge(kb.s_ve.h, ve_at[par])
            for k in range(NB):
                for half in range(2):
                    ii = t.matmul(pq[half][:, :],
                                  ws[:, k * 128:(k + 1) * 128],
                                  xb[:, k * TL + half * 512:
                                     k * TL + half * 512 + 512],
                                  start=(k == 0), stop=(k == NB - 1))
                if k == NB - 1:
                    kb.s_mm.inc(ii)
            mm_at[par] = kb.s_mm.n
            stg = stb[:, par * TL:(par + 1) * TL]
            v.wait_ge(kb.s_mm.h, mm_at[par])
            if st_at[par]:
                v.wait_ge(sst[par].h, st_at[par])
            for half in range(2):
                kb.s_ve.inc(v.tensor_copy(
                    stg[:, half * 512:half * 512 + 512], pq[half][:, :]))
            ve_at[par] = kb.s_ve.n
            sy.wait_ge(kb.s_ve.h, ve_at[par])
            sst[par].inc(sy.dma_start(
                out=dst[dst_row0 + ob * 128: dst_row0 + (ob + 1) * 128, :],
                in_=stg), 16)
            st_at[par] = sst[par].n

    proj_fm(wqp, 8, qk1_in, 0)
    proj_fm(wkp, 8, qk1_in, 1024)
    g.wait_ge(sst[0].h, sst[0].n)
    g.wait_ge(sst[1].h, sst[1].n)
    cc_qk = kb.cc_sem()
    cc_qk.inc(g.collective_compute(
        "AllGather", ALU.bypass, replica_groups=GROUPS,
        ins=[qk1_in[:, :]], outs=[qk1_out[:, :]]), 1)

    for ob in range(8):
        par = ob % 2
        ws = wst[:, par * HID:(par + 1) * HID]
        if mm_at[par]:
            g.wait_ge(kb.s_mm.h, mm_at[par])
        swp[par].inc(g.dma_start(out=ws, in_=wsp[ob]), 16)
        swp[par].wait(t)
        pq = (kb.ps[4 * par + 0], kb.ps[4 * par + 1])
        if ve_at[par]:
            t.wait_ge(kb.s_ve.h, ve_at[par])
        for k in range(NB):
            for half in range(2):
                ii = t.matmul(pq[half][:, :],
                              ws[:, k * 128:(k + 1) * 128],
                              xb[:, k * TL + half * 512:
                                 k * TL + half * 512 + 512],
                              start=(k == 0), stop=(k == NB - 1))
            if k == NB - 1:
                kb.s_mm.inc(ii)
        mm_at[par] = kb.s_mm.n
        v.wait_ge(kb.s_mm.h, mm_at[par])
        if st_at[par]:
            v.wait_ge(sst[par].h, st_at[par])
        sstp = sstg[:, par * 1024:(par + 1) * 1024]
        for half in range(2):
            kb.s_ve.inc(v.tensor_scalar(
                sstp[:, half * 512:half * 512 + 512],
                pq[half][:, :], 32.0, -32.0, ALU.min, ALU.max))
        ve_at[par] = kb.s_ve.n
        sy.wait_ge(kb.s_ve.h, ve_at[par])
        sst[par].inc(sy.dma_start(
            out=s1_in[ob * 128:(ob + 1) * 128, :], in_=sstp), 16)
        st_at[par] = sst[par].n

    g.wait_ge(sst[0].h, sst[0].n)
    g.wait_ge(sst[1].h, sst[1].n)
    cc_s = kb.cc_sem()
    cc_s.inc(g.collective_compute(
        "AllGather", ALU.bypass, replica_groups=GROUPS,
        ins=[s1_in[:, :]], outs=[s1_out[:, :]]), 1)

    t.wait_ge(kb.s_ve.h, kb.s_ve.n)
    vv_prev = 0
    for ch in range(4):
        g.wait_ge(kb.s_mm.h, kb.s_mm.n)
        swp[0].inc(g.dma_start(out=wvs, in_=wvp[ch]), 16)
        swp[0].wait(t)
        if ch > 0:
            t.wait_ge(kb.s_ve.h, vv_prev)
        for k in range(NB):
            for tb in range(8):
                ii = t.matmul(kb.ps[tb][:, :],
                              xb[:, k * TL + tb * 128:
                                 k * TL + tb * 128 + 128],
                              wvs[:, k * 512:(k + 1) * 512],
                              start=(k == 0), stop=(k == NB - 1))
            kb.s_mm.inc(ii)
        v.wait_ge(kb.s_mm.h, kb.s_mm.n)
        v.wait_ge(sst[0].h, sst[0].n)
        v.wait_ge(sst[1].h, sst[1].n)
        for tb in range(8):
            kb.s_ve.inc(v.tensor_copy(stb[:, tb * 512:(tb + 1) * 512],
                                      kb.ps[tb][:, :]))
        vv_prev = kb.s_ve.n
        sy.wait_ge(kb.s_ve.h, vv_prev)
        sst[0].inc(sy.dma_start(
            out=v1_in.rearrange("(tb p) c -> p tb c", p=128)[
                :, :, ch * 512:(ch + 1) * 512],
            in_=stb.rearrange("p (tb c) -> p tb c", tb=8)), 16)

    g.wait_ge(sst[0].h, sst[0].n)
    mm_vl = kb.s_mm.n
    cc_v = kb.cc_sem()
    cc_v.inc(g.collective_compute(
        "AllGather", ALU.bypass, replica_groups=GROUPS,
        ins=[v1_in[:, :]], outs=[v1_out[:, :]]), 1)

    qk1_r = qk1_out.rearrange("(r4 rows) t -> rows r4 t", rows=2048)
    s1_r = s1_out.rearrange("(r4 rows) t -> rows r4 t", rows=1024)
    v1_r3 = v1_out.rearrange("(tb p) c -> p tb c", p=128)
    s.wait_ge(cc_qk.h, 1)
    s.wait_ge(cc_s.h, 1)
    s.wait_ge(kb.s_mm.h, mm_vl)
    s.wait_ge(sst[0].h, sst[0].n)
    s.wait_ge(sst[1].h, sst[1].n)
    pid_z = s.partition_id()
    qrow0 = (pid_z % 4) * 256
    svr.inc(kb.dma(s, out=k12.rearrange("p (r4 t) -> p r4 t", r4=4),
                   in_=qk1_r[bass.ds(1024 + qrow0, 128)]), 16)
    svr.inc(kb.dma(s, out=s12.rearrange("p (r4 t) -> p r4 t", r4=4),
                   in_=s1_r[bass.ds(qrow0, 128)]), 16)

    mv = kb.s_ve.n
    ve_at = [mv, mv]
    v.wait_ge(sst[0].h, sst[0].n)
    st_at = [0, 0]
    proj_fm(wgp, NB, g1park, 0)
    mm_gp = kb.s_mm.n
    s.wait_ge(cc_v.h, 1)
    s.wait_ge(kb.s_mm.h, mm_gp)
    s.wait_ge(sst[0].h, sst[0].n)
    s.wait_ge(sst[1].h, sst[1].n)
    svr.inc(kb.dma(s, out=v12.rearrange("p (tb c) -> p tb c", c=256),
                   in_=v1_r3[:, :, bass.ds((pid_z % 4) * 512, 256)]), 16)
    svr.inc(kb.dma(s, out=q12.rearrange("p (r4 t) -> p r4 t", r4=4),
                   in_=qk1_r[bass.ds(qrow0, 128)]), 16)

    sob.inc(g.dma_start(out=gnr[:, :], in_=gnw), 16)
    sob.wait(t)
    t.wait_ge(kb.s_ve.h, kb.s_ve.n)
    kb.s_mm.inc(t.matmul(kb.ps[6][:, 0:128], kb.ones_row(BF16), gnr[0:1, :],
                         start=True, stop=True))
    v.wait_ge(kb.s_mm.h, kb.s_mm.n)
    kb.s_ve.inc(v.tensor_copy(gnwb[:, :], kb.ps[6][:, 0:128]))

    idf = kb.cs(C_IDENT, F32)
    idb = kb.cs(C_IDENT, BF16)
    trilA_f = kb.cs(C_TRILA, F32)[:, 0:128]
    trilA_b = kb.cs(C_TRILA, BF16)[:, 0:128]
    ones_f = kb.cs(C_ONES, F32)

    cc_os = []
    for pg_i in range(2):
        if pg_i == 0:
            svr.wait(t)
            svr.wait(s)
            svr.wait(v)
            g.wait_ge(cc_qk.h, 1)
            g.wait_ge(cc_v.h, 1)
            for i4 in nc_switch4(kb):
                qrow1 = i4 * 256 + 128
                g.dma_start(
                    out=q12b.rearrange("p (r4 t) -> p r4 t", r4=4),
                    in_=qk1_r[qrow1:qrow1 + 128]).then_inc(svr2.h, 16)
                g.dma_start(
                    out=k12b.rearrange("p (r4 t) -> p r4 t", r4=4),
                    in_=qk1_r[1024 + qrow1:1024 + qrow1 + 128]).then_inc(
                        svr2.h, 16)
                colv1 = i4 * 512 + 256
                g.dma_start(
                    out=v12b.rearrange("p (tb c) -> p tb c", c=256),
                    in_=v1_r3[:, :, colv1:colv1 + 256]).then_inc(svr2.h, 16)
            svr2.n += 3 * 16
            q12c, k12c, v12c = q12, k12, v12
        else:
            g.wait_ge(kb.s_mm.h, kb.s_mm.n)
            g.wait_ge(cc_s.h, 1)
            for i4 in nc_switch4(kb):
                qrow1 = i4 * 256 + 128
                g.dma_start(
                    out=s12.rearrange("p (r4 t) -> p r4 t", r4=4),
                    in_=s1_r[qrow1:qrow1 + 128]).then_inc(svr2.h, 16)
            svr2.n += 16
            svr2.wait(t)
            svr2.wait(s)
            svr2.wait(v)
            q12c, k12c, v12c = q12b, k12b, v12b
        for pp in range(2):
            po = pp * 64
            pv_off = pp * 128
            HKf = HKf_t[po:po + 64, :]
            HVf = HVf_t[po:po + 64, :]
            HKs = HKs_t[po:po + 64, :]
            HVs = HVs_t[po:po + 64, :]
            WT = WT_t[po:po + 64, :]
            ETp = ET_t[po:po + 64, :]
            s.wait_ge(kb.s_mm.h, kb.s_mm.n)
            kb.s_ac.inc(s.activation(ETp[:, :], s12[po:po + 64, :], AF.Exp))
            v.wait_ge(sst[0].h, sst[0].n)
            kb.s_ve.inc(v.memset(zc[:, :], 0.0))
            kb.s_ve.inc(v.memset(HKf[:, :], 0.0))
            kb.s_ve.inc(v.memset(HVf[:, :], 0.0))
            for c in range(16):
                for tb2 in range(2):
                    t.wait_ge(kb.s_ac.h, kb.s_ac.n)
                    t.wait_ge(kb.s_ve.h, kb.s_ve.n)
                    kb.s_mm.inc(t.transpose(
                        kb.ps[6][:, 0:64],
                        s12[po:po + 64,
                            c * CH + tb2 * 128: c * CH + tb2 * 128 + 128],
                        idf[po:po + 64, po:po + 64]))
                    s.wait_ge(kb.s_mm.h, kb.s_mm.n)
                    kb.s_ac.inc(s.activation(Ef[:, tb2 * 64:(tb2 + 1) * 64],
                                             kb.ps[6][:, 0:64], AF.Exp))
                    kb.s_ac.inc(s.activation(Eb[:, tb2 * 64:(tb2 + 1) * 64],
                                             kb.ps[6][:, 0:64], AF.Exp))
                    kb.s_mm.inc(t.transpose(
                        ps7b[:, 0:64],
                        k12c[po:po + 64,
                             c * CH + tb2 * 128: c * CH + tb2 * 128 + 128],
                        idb[po:po + 64, po:po + 64]))
                    v.wait_ge(kb.s_mm.h, kb.s_mm.n)
                    kb.s_ve.inc(v.tensor_copy(
                        ktm[:, tb2 * 64:(tb2 + 1) * 64], ps7b[:, 0:64]))
                    t.wait_ge(kb.s_ve.h, kb.s_ve.n)
                s.wait_ge(kb.s_ac.h, kb.s_ac.n)
                t.wait_ge(kb.s_ac.h, kb.s_ac.n)
                t.wait_ge(kb.s_ve.h, kb.s_ve.n)
                for tb2 in range(2):
                    pz = kb.ps[4 + tb2]
                    t.matmul(pz[:, 0:64], trilA_f,
                             Ef[:, tb2 * 64:tb2 * 64 + 64],
                             start=True, stop=False)
                    if tb2 == 1:
                        t.matmul(pz[:, 0:64], ones_f[:, 0:128], Ef[:, 0:64],
                                 start=False, stop=False)
                    ii = t.matmul(pz[:, 0:64], kb.ones_row(F32), zc[0:1, :],
                                  start=False, stop=True)
                    kb.s_mm.inc(ii)
                v.wait_ge(kb.s_mm.h, kb.s_mm.n)
                for tb2 in range(2):
                    kb.s_ve.inc(v.reciprocal(rz[:, tb2 * 64:(tb2 + 1) * 64],
                                             kb.ps[4 + tb2][:, 0:64]))
                kb.vsync()
                kb.s_ve.inc(v.tensor_scalar(rz8[:, :], rz[:, :], 0.125,
                                            None, ALU.mult))
                t.wait_ge(kb.s_ac.h, kb.s_ac.n)
                t.matmul(kb.ps[6][0:1, 0:64], kb.ones_col(F32), Ef[:, 0:64],
                         start=True, stop=False)
                ii = t.matmul(kb.ps[6][0:1, 0:64], kb.ones_col(F32),
                              Ef[:, 64:128], start=False, stop=True)
                kb.s_mm.inc(ii)
                v.wait_ge(kb.s_mm.h, kb.s_mm.n)
                kb.s_ve.inc(v.tensor_copy(csum[:, :], kb.ps[6][0:1, 0:64]))
                kb.vsync()
                kb.s_ve.inc(v.tensor_add(zc[:, :], zc[:, :], csum[:, :]))
                for jb in range(2):
                    ii = t.matmul(kb.ps[2 + jb][:, 0:256],
                                  k12c[po:po + 64,
                                       c * CH + jb * 128:
                                       c * CH + jb * 128 + 128],
                                  q12c[po:po + 64, c * CH:(c + 1) * CH],
                                  start=True, stop=True)
                    kb.s_mm.inc(ii)
                v.wait_ge(kb.s_mm.h, kb.s_mm.n)
                kb.s_ve.inc(v.tensor_mul(A_sb[:, 0:128], kb.ps[2][:, 0:128],
                                         trilA_b))
                kb.s_ve.inc(v.tensor_copy(A_sb[:, 128:256],
                                          kb.ps[2][:, 128:256]))
                kb.s_ve.inc(v.memset(A_sb[:, 256:384], 0.0))
                kb.s_ve.inc(v.tensor_mul(A_sb[:, 384:512],
                                         kb.ps[3][:, 128:256], trilA_b))
                kb.s_ve.inc(v.tensor_copy(HKs[:, :], HKf[:, :]))
                kb.s_ve.inc(v.tensor_copy(HVs[:, :], HVf[:, :]))
                snap_ve = kb.s_ve.n
                t.wait_ge(kb.s_ve.h, snap_ve)
                for jb in range(2):
                    i1 = t.matmul(kb.ps[0][0:64, 0:64],
                                  ktm[:, jb * 64:(jb + 1) * 64],
                                  Eb[:, jb * 64:(jb + 1) * 64],
                                  start=(jb == 0), stop=(jb == 1))
                    i2 = t.matmul(kb.ps[1][0:64, 0:128],
                                  Eb[:, jb * 64:(jb + 1) * 64],
                                  v12c[:, (c * 2 + jb) * 256 + pv_off:
                                       (c * 2 + jb) * 256 + pv_off + 128],
                                  start=(jb == 0), stop=(jb == 1))
                kb.s_mm.inc(i1)
                kb.s_mm.inc(i2)
                hk_mm = kb.s_mm.n
                v.wait_ge(kb.s_mm.h, hk_mm)
                kb.s_ve.inc(v.tensor_add(HKf[:, :], kb.ps[0][0:64, 0:64],
                                         HKf[:, :]))
                kb.s_ve.inc(v.tensor_add(HVf[:, :], kb.ps[1][0:64, 0:128],
                                         HVf[:, :]))
                t.wait_ge(kb.s_ve.h, kb.s_ve.n)
                for tb2 in range(2):
                    pok = kb.ps[4 + tb2]
                    t.matmul(pok[:, 0:64],
                             A_sb[:, tb2 * 128:tb2 * 128 + 128],
                             Eb[:, 0:64], start=True, stop=False)
                    t.matmul(pok[:, 0:64],
                             A_sb[:, 256 + tb2 * 128:256 + tb2 * 128 + 128],
                             Eb[:, 64:128], start=False, stop=False)
                    ii = t.matmul(pok[:, 0:64],
                                  q12c[po:po + 64,
                                       c * CH + tb2 * 128:
                                       c * CH + tb2 * 128 + 128],
                                  HKs[:, :], start=False, stop=True)
                    kb.s_mm.inc(ii)
                v.wait_ge(kb.s_mm.h, kb.s_mm.n)
                for tb2 in range(2):
                    kb.s_ve.inc(v.tensor_tensor(
                        okf[:, tb2 * 64:(tb2 + 1) * 64],
                        kb.ps[4 + tb2][:, 0:64],
                        rz8[:, tb2 * 64:(tb2 + 1) * 64], ALU.mult))
                kb.vsync()
                nm = msq[:, 0:1]
                sm = msq[:, 1:2]
                for tb2 in range(2):
                    ok2 = okf[:, tb2 * 64:(tb2 + 1) * 64]
                    kb.s_ve.inc(v.reduce_max(nm, ok2, mybir.AxisListType.X,
                                             negate=True))
                    kb.vsync()
                    s.wait_ge(kb.s_ve.h, kb.s_ve.n)
                    kb.s_ac.inc(s.activation(pvb[:, tb2 * 64:(tb2 + 1) * 64],
                                             ok2, AF.Exp, bias=nm,
                                             accum_out=sm))
                    v.wait_ge(kb.s_ac.h, kb.s_ac.n)
                    kb.s_ve.inc(v.reciprocal(sm, sm))
                    kb.vsync()
                    kb.s_ve.inc(v.tensor_scalar(
                        pvb[:, tb2 * 64:(tb2 + 1) * 64],
                        pvb[:, tb2 * 64:(tb2 + 1) * 64], sm, None, ALU.mult))
                    kb.vsync()
                kb.s_ve.inc(v.tensor_tensor(Wf[:, :], pvb[:, :], rz[:, :],
                                            ALU.mult))
                kb.vsync()
                for tb2 in range(2):
                    t.wait_ge(kb.s_ve.h, kb.s_ve.n)
                    kb.s_mm.inc(t.transpose(
                        ps7b[0:64, 0:128],
                        Wf[:, tb2 * 64:(tb2 + 1) * 64], idb))
                    v.wait_ge(kb.s_mm.h, kb.s_mm.n)
                    kb.s_ve.inc(v.tensor_copy(
                        WT[:, tb2 * 128:(tb2 + 1) * 128],
                        ps7b[0:64, 0:128]))
                    t.wait_ge(kb.s_ve.h, kb.s_ve.n)
                for jb in range(2):
                    ii = t.matmul(kb.ps[2 + jb][:, 0:256],
                                  ETp[:, c * CH + jb * 128:
                                      c * CH + jb * 128 + 128],
                                  WT[:, 0:256],
                                  start=True, stop=True)
                    kb.s_mm.inc(ii)
                v.wait_ge(kb.s_mm.h, kb.s_mm.n)
                kb.s_ve.inc(v.tensor_mul(U_sb[:, 0:128], kb.ps[2][:, 0:128],
                                         trilA_b))
                kb.s_ve.inc(v.tensor_copy(U_sb[:, 128:256],
                                          kb.ps[2][:, 128:256]))
                kb.s_ve.inc(v.memset(U_sb[:, 256:384], 0.0))
                kb.s_ve.inc(v.tensor_mul(U_sb[:, 384:512],
                                         kb.ps[3][:, 128:256], trilA_b))
                tve_U = kb.s_ve.n
                t.wait_ge(kb.s_ve.h, tve_U)
                for tb2 in range(2):
                    pov = kb.ps[4 + tb2]
                    t.matmul(pov[:, 0:128],
                             U_sb[:, tb2 * 128:tb2 * 128 + 128],
                             v12c[:, (c * 2 + 0) * 256 + pv_off:
                                  (c * 2 + 0) * 256 + pv_off + 128],
                             start=True, stop=False)
                    t.matmul(pov[:, 0:128],
                             U_sb[:, 256 + tb2 * 128:256 + tb2 * 128 + 128],
                             v12c[:, (c * 2 + 1) * 256 + pv_off:
                                  (c * 2 + 1) * 256 + pv_off + 128],
                             start=False, stop=False)
                    ii = t.matmul(pov[:, 0:128],
                                  WT[:, tb2 * 128:(tb2 + 1) * 128],
                                  HVs[:, :], start=False, stop=True)
                    kb.s_mm.inc(ii)
                mm_ov = kb.s_mm.n
                v.wait_ge(kb.s_mm.h, mm_ov)
                for tb2 in range(2):
                    ovb = ovf[:, tb2 * 128:(tb2 + 1) * 128]
                    kb.s_ve.inc(v.tensor_copy(ovb, kb.ps[4 + tb2][:, 0:128]))
                    kb.vsync()
                    ms = msq[:, 2:3]
                    kb.s_ve.inc(v.scalar_tensor_tensor(
                        ov2[:, :], ovb, 1.0, ovb, ALU.mult, ALU.mult,
                        accum_out=ms))
                    kb.vsync()
                    kb.s_ve.inc(v.tensor_scalar(ms, ms, 1.0 / 128, 1e-5,
                                                ALU.mult, ALU.add))
                    kb.vsync()
                    s.wait_ge(kb.s_ve.h, kb.s_ve.n)
                    kb.s_ac.inc(s.activation(ms, ms, AF.Sqrt))
                    v.wait_ge(kb.s_ac.h, kb.s_ac.n)
                    kb.s_ve.inc(v.reciprocal(ms, ms))
                    kb.vsync()
                    kb.s_ve.inc(v.scalar_tensor_tensor(
                        onb[:, :], ovb, ms, gnwb[:, :], ALU.mult, ALU.mult))
                    kb.vsync()
                    t.wait_ge(kb.s_ve.h, kb.s_ve.n)
                    kb.s_mm.inc(t.transpose(ps7b[:, 0:128], onb[:, :], idb))
                    v.wait_ge(kb.s_mm.h, kb.s_mm.n)
                    kb.s_ve.inc(v.tensor_copy(
                        oT[:, c * CH + tb2 * 128: c * CH + tb2 * 128 + 128],
                        ps7b[:, 0:128]))
                    t.wait_ge(kb.s_ve.h, kb.s_ve.n)
            sy.wait_ge(kb.s_ve.h, kb.s_ve.n)
            pbk = pg_i * 2 + pp
            sst[0].inc(sy.dma_start(
                out=o1_in[pbk * 128: pbk * 128 + 128, :],
                in_=oT[:, :]), 16)
            g.wait_ge(sst[0].h, sst[0].n)
            cco = kb.cc_sem()
            cco.inc(g.collective_compute(
                "AllGather", ALU.bypass, replica_groups=GROUPS,
                ins=[o1_in[pbk * 128: pbk * 128 + 128, :]],
                outs=[o1_oS[pbk][:, :]]), 1)
            cc_os.append(cco)

    for cco in cc_os:
        g.wait_ge(cco.h, 1)
    kb.s_go.inc(g.engine_nop())

    g.wait_ge(kb.s_go.h, kb.s_go.n)
    g.wait_ge(kb.s_mm.h, kb.s_mm.n)
    g.wait_ge(kb.s_ve.h, kb.s_ve.n)
    for i4 in nc_switch4(kb):
        tok0 = i4 * TL
        for sN in range(4):
            g.dma_start(
                out=o_loc1.rearrange("p (r4 s4 t) -> p r4 s4 t",
                                     r4=4, s4=4)[:, :, sN],
                in_=o1_oS[sN].rearrange("(r4 p) t -> p r4 t", p=128)[
                    :, :, tok0:tok0 + TL]).then_inc(sol.h, 16)
    sol.n += 4 * 16
    gate_ve = []
    for pb in range(NB):
        par = pb % 2
        gf = gf1[:, par * TL:(par + 1) * TL]
        if pb >= 2:
            sy.wait_ge(kb.s_ac.h, kb.s_ac.n)
        sgf[par].inc(sy.dma_start(
            out=gf, in_=g1park[pb * 128:(pb + 1) * 128, :]), 16)
        sgf[par].wait(s)
        sgg = sg1[:, par * TL:(par + 1) * TL]
        if pb >= 2:
            s.wait_ge(kb.s_ve.h, gate_ve[pb - 2])
        kb.s_ac.inc(s.activation(sgg, gf, AF.Silu))
        v.wait_ge(kb.s_ac.h, kb.s_ac.n)
        if pb == 0:
            v.wait_ge(sol.h, sol.n)
        kb.s_ve.inc(v.tensor_tensor(o_loc1[:, pb * TL:(pb + 1) * TL],
                                    o_loc1[:, pb * TL:(pb + 1) * TL],
                                    sgg, ALU.mult))
        gate_ve.append(kb.s_ve.n)
    og_done = kb.s_ve.n

    state = {"wo_at": [0, 0], "ve_bank": [0, 0, 0, 0]}

    def src_of(ob, tcx):
        par = ob % 2
        bank = ob % 4
        wos = wo1[:, par * HID:(par + 1) * HID]
        if state["wo_at"][par]:
            g.wait_ge(kb.s_mm.h, state["wo_at"][par])
        swo[par].inc(g.dma_start(out=wos, in_=wop[ob]), 16)
        swo[par].wait(t)
        if ob == 0 and tcx == 0:
            t.wait_ge(kb.s_ve.h, og_done)
        if state["ve_bank"][bank]:
            t.wait_ge(kb.s_ve.h, state["ve_bank"][bank])
        for kk in range(NB):
            ii = t.matmul(kb.ps[bank][:, :],
                          wos[:, kk * 128:(kk + 1) * 128],
                          o_loc1[:, kk * TL + tcx * 512:
                                 kk * TL + tcx * 512 + 512],
                          start=(kk == 0), stop=(kk == NB - 1))
        kb.s_mm.inc(ii)
        state["wo_at"][par] = kb.s_mm.n
        state["ve_bank"][bank] = kb.s_ve.n + 1
        return kb.ps[bank][:, :], [(kb.s_mm, kb.s_mm.n)]

    emit_postln(kb, src_of, x_read, x_writes, last=last,
                y_of=lambda ob, tcx: ep_y_l1a[:, ob * TC:(ob + 1) * TC],
                xin_buf=ep_xin_l1a,
                aux={"y2": y2_l1a, "stat": stat_l1a,
                     "mub": mub_l1a, "rsb": rsb_l1a})

def emit_l1_ffn(kb, x_read, x_writes, last=False):
    t, v, s, g, sy = kb.t, kb.v, kb.s, kb.g, kb.sy
    xb = kb.sb["xb"]
    HB1 = 11
    gw = kb.inp("l1f_gw", [128, NB * 8])
    ebias = kb.inp("l1f_eb", [1, 8])
    w1 = kb.inp("l1f_w1", [8, HB1, 128, HID])
    w3 = kb.inp("l1f_w3", [8, HB1, 128, HID])
    w2 = kb.inp("l1f_w2", [8, NB, 128, 1408])
    sw1 = kb.inp("l1f_sw1", [HB1, 128, HID])
    sw3 = kb.inp("l1f_sw3", [HB1, 128, HID])
    sw2 = kb.inp("l1f_sw2", [NB, 128, 1408])

    AR, ARF = kb.AR, kb.ARF
    accd = kb.scratch("l1f_accd", [HID, TL], F32)
    xe = AR[:, 0:16384]
    h1 = AR[:, 16384:27648]
    sg1 = AR[:, 27648:29696]
    wa_sb = AR[:, 29696:33792]
    wb_sb = AR[:, 33792:37888]
    wc_sb = AR[:, 37888:40704]
    wa_slots = [AR[:, 29696 + i * 2048: 29696 + (i + 1) * 2048]
                for i in range(2)] + \
               [AR[:, 53248 + i * 2048: 53248 + (i + 1) * 2048]
                for i in range(2)]
    wb_slots = [AR[:, 33792 + i * 2048: 33792 + (i + 1) * 2048]
                for i in range(2)] + \
               [AR[:, 57344 + i * 2048: 57344 + (i + 1) * 2048]
                for i in range(2)]
    wc_slots = [AR[:, 37888 + i * 1408: 37888 + (i + 1) * 1408]
                for i in range(2)] + \
               [AR[:, 61440 + i * 1408: 61440 + (i + 1) * 1408]
                for i in range(2)]
    wbc = AR[:, 40704:41728]
    wrows = AR[0:8, 41728:42752]
    wrow = AR[0:1, 42752:43776]
    gw_sb2 = ARF[:, 22144:22272]
    xf = ARF[:, 22272:23296]
    sc_sb = ARF[:, 23296:23360]
    bb_sb = ARF[:, 23360:23368]
    t8 = ARF[:, 23368:23416]
    ebr = ARF[0:1, 23416:23424]
    we_sb = ARF[:, 23424:23488]
    dstg = ARF[:, 23552:25600]
    srcb = ARF[:, 25600:26624]
    ep_y1 = ARF[:, 26624:34816]
    ep_xin1 = ARF[:, 34816:35840]
    ep_aux1 = {"y2": AR[:, 43776:44800], "stat": ARF[0:1, 35840:37376],
               "mub": ARF[:, 37376:37888], "rsb": ARF[:, 37888:38400]}

    kb.barrier()
    sacc = [kb.sem("l1f_a0"), kb.sem("l1f_a1")]
    sdl = [kb.sem("l1f_d0"), kb.sem("l1f_d1")]
    sxf = [kb.sem("l1f_x0"), kb.sem("l1f_x1")]
    swp = [kb.sem(f"l1f_wp{i}") for i in range(4)]
    sw2p = [kb.sem(f"l1f_wq{i}") for i in range(4)]
    sgw = kb.sem("l1f_gws")
    srow = kb.sem("l1f_row")

    gw_sb = gw_sb2
    sgw.inc(sy.dma_start(out=gw_sb[:, :], in_=gw), 16)
    sgw.inc(sy.dma_start(out=ebr[:, :], in_=ebias), 16)
    cc_i = 0
    for k in range(NB):
        for hf in range(2):
            par = cc_i % 2
            cc_i += 1
            xfb = xf[:, par * TC:(par + 1) * TC]
            if cc_i > 2:
                sy.wait_ge(kb.s_mm.h, kb.s_mm.n)
            sxf[par].inc(sy.dma_start(
                out=xfb, in_=x_read[k * 128:(k + 1) * 128,
                                    hf * TC:(hf + 1) * TC]), 16)
            sxf[par].wait(t)
            if k == 0:
                sgw.wait(t)
            for tl4 in range(4):
                tile = hf * 4 + tl4
                ii = t.matmul(kb.ps[tile][:, 0:8],
                              xfb[:, tl4 * 128:(tl4 + 1) * 128],
                              gw_sb[:, k * 8:(k + 1) * 8],
                              start=(k == 0), stop=(k == NB - 1))
            kb.s_mm.inc(ii)
    sc_mm = kb.s_mm.n
    s.wait_ge(kb.s_mm.h, sc_mm)
    for tile in range(8):
        kb.s_ac.inc(s.activation(sc_sb[:, tile * 8:(tile + 1) * 8],
                                 kb.ps[tile][:, 0:8], AF.Sigmoid))
    t.wait_ge(kb.s_ac.h, kb.s_ac.n)
    kb.s_mm.inc(t.matmul(kb.ps[5][:, 0:8], kb.ones_row(F32), ebr[0:1, :],
                         start=True, stop=True))
    v.wait_ge(kb.s_mm.h, kb.s_mm.n)
    kb.s_ve.inc(v.tensor_copy(bb_sb[:, :], kb.ps[5][:, 0:8]))
    v.wait_ge(kb.s_ac.h, kb.s_ac.n)
    iom = kb.cs(C_IOTA8M, F32)
    for tile in range(8):
        sct = sc_sb[:, tile * 8:(tile + 1) * 8]
        bsc = t8[:, 0:8]
        m1 = t8[:, 8:9]
        eq = t8[:, 16:24]
        tt1 = t8[:, 24:32]
        i1 = t8[:, 9:10]
        oh1 = t8[:, 32:40]
        ohh = t8[:, 40:48]
        kb.vsync()
        kb.s_ve.inc(v.tensor_add(bsc, sct, bb_sb[:, :]))
        kb.vsync()
        kb.s_ve.inc(v.reduce_max(m1, bsc, mybir.AxisListType.X))
        kb.vsync()
        kb.s_ve.inc(v.tensor_single_scalar(eq, bsc, m1, ALU.is_equal))
        kb.vsync()
        kb.s_ve.inc(v.tensor_mul(tt1, eq, iom))
        kb.vsync()
        kb.s_ve.inc(v.tensor_reduce(i1, tt1, mybir.AxisListType.X, ALU.min))
        kb.vsync()
        kb.s_ve.inc(v.tensor_single_scalar(oh1, iom, i1, ALU.is_equal))
        kb.vsync()
        kb.s_ve.inc(v.scalar_tensor_tensor(bsc, oh1, -1e9, bsc,
                                           ALU.mult, ALU.add))
        kb.vsync()
        kb.s_ve.inc(v.reduce_max(m1, bsc, mybir.AxisListType.X))
        kb.vsync()
        kb.s_ve.inc(v.tensor_single_scalar(eq, bsc, m1, ALU.is_equal))
        kb.vsync()
        kb.s_ve.inc(v.tensor_mul(tt1, eq, iom))
        kb.vsync()
        kb.s_ve.inc(v.tensor_reduce(i1, tt1, mybir.AxisListType.X, ALU.min))
        kb.vsync()
        kb.s_ve.inc(v.tensor_single_scalar(eq, iom, i1, ALU.is_equal))
        kb.vsync()
        kb.s_ve.inc(v.tensor_add(ohh, oh1, eq))
        kb.vsync()
        kb.s_ve.inc(v.tensor_mul(we_sb[:, tile * 8:(tile + 1) * 8],
                                 sct, ohh))
    idf = kb.cs(C_IDENT, F32)
    for tile in range(8):
        t.wait_ge(kb.s_ve.h, kb.s_ve.n)
        kb.s_mm.inc(t.transpose(kb.ps[6][0:8, 0:128],
                                we_sb[:, tile * 8:(tile + 1) * 8], idf))
        v.wait_ge(kb.s_mm.h, kb.s_mm.n)
        kb.s_ve.inc(v.tensor_copy(wrows[0:8, tile * 128:(tile + 1) * 128],
                                  kb.ps[6][0:8, 0:128]))
        t.wait_ge(kb.s_ve.h, kb.s_ve.n)

    def run_expert(w1p, w3p, w2p, rhs_buf, add, prep=None):
        t.wait_ge(kb.s_ve.h, kb.s_ve.n)
        mm_at = [0, 0, 0, 0]
        ac_at = [0, 0]
        ve_at = [0, 0]
        for hb in range(HB1):
            p4 = hb % 4
            par = hb % 2
            was = wa_slots[p4]
            wbs = wb_slots[p4]
            if mm_at[p4]:
                g.wait_ge(kb.s_mm.h, mm_at[p4])
            swp[p4].inc(g.dma_start(out=was, in_=w1p[hb]), 16)
            swp[p4].inc(g.dma_start(out=wbs, in_=w3p[hb]), 16)
            pg = (kb.ps[4 * par + 0], kb.ps[4 * par + 1])
            pu = (kb.ps[4 * par + 2], kb.ps[4 * par + 3])
            swp[p4].wait(t)
            if ac_at[par]:
                t.wait_ge(kb.s_ac.h, ac_at[par])
                t.wait_ge(kb.s_ve.h, ve_at[par])
            for half in range(2):
                for k in range(NB):
                    rhs = rhs_buf[:, k * TL + half * 512:
                                  k * TL + half * 512 + 512]
                    st, sp = (k == 0), (k == NB - 1)
                    i1 = t.matmul(pg[half][:, :],
                                  was[:, k * 128:(k + 1) * 128], rhs,
                                  start=st, stop=sp)
                    i2 = t.matmul(pu[half][:, :],
                                  wbs[:, k * 128:(k + 1) * 128], rhs,
                                  start=st, stop=sp)
                    if sp:
                        kb.s_mm.inc(i1)
                        kb.s_mm.inc(i2)
            mm_at[p4] = kb.s_mm.n
            hs = h1[:, hb * TL:(hb + 1) * TL]
            sgs = sg1[:, par * TL:(par + 1) * TL]
            s.wait_ge(kb.s_mm.h, mm_at[p4])
            for half in range(2):
                kb.s_ac.inc(s.activation(sgs[:, half * 512:half * 512 + 512],
                                         pg[half][:, :], AF.Silu))
            ac_at[par] = kb.s_ac.n
            v.wait_ge(kb.s_ac.h, ac_at[par])
            for half in range(2):
                kb.s_ve.inc(v.tensor_tensor(
                    hs[:, half * 512:half * 512 + 512], pu[half][:, :],
                    sgs[:, half * 512:half * 512 + 512], ALU.mult))
            ve_at[par] = kb.s_ve.n
        mm_gu = kb.s_mm.n
        ac_gu = kb.s_ac.n
        ve_gu = kb.s_ve.n
        if prep is not None:
            prep(mm_gu)
        t.wait_ge(kb.s_ac.h, ac_gu)
        t.wait_ge(kb.s_ve.h, ve_gu)
        dmm = [0, 0, 0, 0]
        dve = [0, 0]
        for ob in range(NB):
            p4 = ob % 4
            par = ob % 2
            wcs = wc_slots[p4]
            if dmm[p4]:
                g.wait_ge(kb.s_mm.h, dmm[p4])
            sw2p[p4].inc(g.dma_start(out=wcs, in_=w2p[ob]), 16)
            pd = (kb.ps[4 * par + 0], kb.ps[4 * par + 1])
            sw2p[p4].wait(t)
            if dve[par]:
                t.wait_ge(kb.s_ve.h, dve[par])
            for half in range(2):
                for k in range(HB1):
                    i1 = t.matmul(pd[half][:, :],
                                  wcs[:, k * 128:(k + 1) * 128],
                                  h1[:, k * TL + half * 512:
                                     k * TL + half * 512 + 512],
                                  start=(k == 0), stop=(k == HB1 - 1))
                    if k == HB1 - 1:
                        kb.s_mm.inc(i1)
            dmm[p4] = kb.s_mm.n
            v.wait_ge(kb.s_mm.h, dmm[p4])
            if sacc[par].n:
                v.wait_ge(sacc[par].h, sacc[par].n)
            ds_ = dstg[:, par * TL:(par + 1) * TL]
            for half in range(2):
                kb.s_ve.inc(v.tensor_copy(
                    ds_[:, half * 512:half * 512 + 512], pd[half][:, :]))
            dve[par] = kb.s_ve.n
            g.wait_ge(kb.s_ve.h, dve[par])
            if sacc[par].n:
                g.wait_ge(sacc[par].h, sacc[par].n)
            sacc[par].inc(g.dma_start(
                out=accd[ob * 128:(ob + 1) * 128, :], in_=ds_,
                accum_op=(ALU.add if add else ALU.bypass)), 16)
        t.wait_ge(kb.s_ve.h, kb.s_ve.n)

    ve_wrows = kb.s_ve.n
    bst = {"mm_bc": 0}

    def mk_prep(e):
        def prep(mm_gu):
            if e == 0:
                g.wait_ge(kb.s_ve.h, ve_wrows)
            if bst["mm_bc"]:
                g.wait_ge(kb.s_mm.h, bst["mm_bc"])
            srow.inc(g.dma_start(out=wrow[:, :], in_=wrows[e:e + 1, :]), 16)
            srow.wait(t)
            t.wait_ge(kb.s_ve.h, kb.s_ve.n)
            t.wait_ge(kb.s_ac.h, kb.s_ac.n)
            for half in range(2):
                kb.s_mm.inc(t.matmul(kb.ps[6 + half][:, :],
                                     kb.ones_row(BF16),
                                     wrow[0:1, half * 512:half * 512 + 512],
                                     start=True, stop=True))
            bst["mm_bc"] = kb.s_mm.n
            v.wait_ge(kb.s_mm.h, bst["mm_bc"])
            for half in range(2):
                kb.s_ve.inc(v.tensor_copy(
                    wbc[:, half * 512:half * 512 + 512],
                    kb.ps[6 + half][:, :]))
            kb.vsync()
            v.wait_ge(kb.s_mm.h, mm_gu)
            for k in range(NB):
                kb.s_ve.inc(v.tensor_tensor(xe[:, k * TL:(k + 1) * TL],
                                            xb[:, k * TL:(k + 1) * TL],
                                            wbc[:, :], ALU.mult))
        return prep

    run_expert(sw1, sw3, sw2, xb, add=False, prep=mk_prep(0))
    for e in range(8):
        run_expert(w1[e], w3[e], w2[e], xe, add=True,
                   prep=mk_prep(e + 1) if e < 7 else None)

    g.wait_ge(sacc[0].h, sacc[0].n)
    g.wait_ge(sacc[1].h, sacc[1].n)
    kb.s_go.inc(g.engine_nop())
    sy.wait_ge(kb.s_go.h, kb.s_go.n)
    ve_src = [0, 0]

    def src_of(ob, tcx):
        par = ob % 2
        sb_ = srcb[:, par * TC:(par + 1) * TC]
        if ve_src[par]:
            sy.wait_ge(kb.s_ve.h, ve_src[par])
        sdl[par].inc(sy.dma_start(
            out=sb_, in_=accd[ob * 128:(ob + 1) * 128,
                              tcx * TC:(tcx + 1) * TC]), 16)
        ve_src[par] = kb.s_ve.n + 1
        return sb_, [(sdl[par], sdl[par].n)]

    emit_postln(kb, src_of, x_read, x_writes, last=last,
                y_of=lambda ob, tcx: ep_y1[:, ob * TC:(ob + 1) * TC],
                xin_buf=ep_xin1, aux=ep_aux1)



_BUILD_CACHE = {}

_EMITTERS = {}


def _get_emitters():
    if not _EMITTERS:
        _EMITTERS.update({"l0a": emit_l0_attn, "l0f": emit_l0_ffn,
                          "l1a": emit_l1_attn, "l1f": emit_l1_ffn})
    return _EMITTERS


def _build_launch(stages):
    key = tuple(stages)
    if key in _BUILD_CACHE:
        return _BUILD_CACHE[key]
    nc = bass.Bass("TRN2", target_bir_lowering=False)
    kb = KB(nc)
    em = _get_emitters()
    all_eng = [mybir.EngineType.PE, mybir.EngineType.DVE,
               mybir.EngineType.Activation, mybir.EngineType.Pool,
               mybir.EngineType.SP]
    with nc.Block() as block:
        with nc.body("prog", valid_engines=all_eng):
            kb.load_consts(C_W if "l1a" in stages else C_TRILA[0])
            x_in = kb.inp("x_in", [HID, TL], F32)
            x_out = kb.outp("x_out", [HID, TL], F32)
            emit_load_xb(kb, x_in)
            n = len(stages)
            for i, st in enumerate(stages):
                is_last = (i == n - 1)
                xr = x_in if i == 0 else kb.scratch("xpark", [HID, TL], F32)
                xw = [x_out] if is_last else [
                    kb.scratch("xpark", [HID, TL], F32)]
                em[st](kb, xr, xw, last=is_last)
            kb.sy.wait_ge(kb.s_st.h, kb.s_st.n)
    _BUILD_CACHE[key] = (nc, kb)
    return nc, kb


def _prep_weights(inputs):
    w = {}
    w["consts"] = _consts_arr()
    if "L0_attn_Wq" in inputs:
        w["l0a_wq"] = _lhsT_pack(np.ascontiguousarray(inputs["L0_attn_Wq"]))
        w["l0a_wk"] = _lhsT_pack(np.ascontiguousarray(inputs["L0_attn_Wk"]))
        w["l0a_wg"] = _lhsT_pack(np.ascontiguousarray(inputs["L0_attn_Wg"]))
        wv = inputs["L0_attn_Wv"]
        w["l0a_wv"] = np.ascontiguousarray(
            wv.reshape(16, 128, 8, 512).transpose(2, 1, 0, 3).reshape(
                8, 128, 16 * 512))
        wo_p = _lhsT_pack(np.ascontiguousarray(inputs["L0_attn_Wo"]))
        perm = [r * 8 + h * 4 + j
                for h in range(2) for r in range(4) for j in range(4)]
        w["l0a_wo"] = np.ascontiguousarray(
            wo_p.reshape(16, 128, 32, 128)[:, :, perm, :].reshape(
                16, 128, 4096))
        wgt = inputs["L0_attn_Wgate"]
        w["l0a_wgate"] = np.ascontiguousarray(
            wgt.reshape(2, 128, 4).transpose(1, 0, 2).reshape(128, 8))
    if "L1_attn_Wq" in inputs:
        w["l1a_wq"] = _lhsT_pack(np.ascontiguousarray(inputs["L1_attn_Wq"]))
        w["l1a_wk"] = _lhsT_pack(np.ascontiguousarray(inputs["L1_attn_Wk"]))
        w["l1a_ws"] = _lhsT_pack(np.ascontiguousarray(inputs["L1_attn_Ws"]))
        w["l1a_wg"] = _lhsT_pack(np.ascontiguousarray(inputs["L1_attn_Wg"]))
        w["l1a_wo"] = _lhsT_pack(np.ascontiguousarray(inputs["L1_attn_Wo"]))
        wv1 = inputs["L1_attn_Wv"]
        w["l1a_wv"] = np.ascontiguousarray(
            wv1.reshape(16, 128, 4, 512).transpose(2, 1, 0, 3).reshape(
                4, 128, 16 * 512))
        w["l1a_gnw"] = np.ascontiguousarray(
            inputs["L1_attn_g_norm_weight"].reshape(1, 128).astype(
                np.float32))
    if "L1_ffn_gate_w" in inputs:
        gwT = np.ascontiguousarray(inputs["L1_ffn_gate_w"].T)
        w["l1f_gw"] = np.ascontiguousarray(
            gwT.reshape(16, 128, 8).transpose(1, 0, 2).reshape(128, 128))
        w["l1f_eb"] = np.ascontiguousarray(
            inputs["L1_ffn_expert_bias"].reshape(1, 8).astype(np.float32))
        w["l1f_w1"] = np.stack([_lhsT_pack(np.ascontiguousarray(
            inputs["L1_ffn_w1"][e].T)) for e in range(8)])
        w["l1f_w3"] = np.stack([_lhsT_pack(np.ascontiguousarray(
            inputs["L1_ffn_w3"][e].T)) for e in range(8)])
        w["l1f_w2"] = np.stack([_lhsT_pack(np.ascontiguousarray(
            inputs["L1_ffn_w2"][e].T)) for e in range(8)])
        w["l1f_sw1"] = _lhsT_pack(np.ascontiguousarray(inputs["L1_ffn_sw1"].T))
        w["l1f_sw3"] = _lhsT_pack(np.ascontiguousarray(inputs["L1_ffn_sw3"].T))
        w["l1f_sw2"] = _lhsT_pack(np.ascontiguousarray(inputs["L1_ffn_sw2"].T))
    if "L0_ffn_w_gate" in inputs:
        w["l0f_wg"] = _lhsT_pack(
            np.ascontiguousarray(inputs["L0_ffn_w_gate"].T))
        w["l0f_wu"] = _lhsT_pack(np.ascontiguousarray(inputs["L0_ffn_w_up"].T))
        wdT = np.ascontiguousarray(inputs["L0_ffn_w_down"].T)
        wdp = _lhsT_pack(wdT)
        w["l0f_wd"] = np.ascontiguousarray(
            wdp.reshape(4, 4, 128, 8192).transpose(0, 2, 1, 3).reshape(
                4, 128, 4 * 8192))
    return w


def run_chain(x, w, launch_groups, percore_extra=None):
    xs = []
    for c in range(NCORES):
        b, q = c // 4, c % 4
        xs.append(np.ascontiguousarray(
            x[b, q * TL:(q + 1) * TL, :].T.astype(np.float32)))

    for stages in launch_groups:
        nc, kb = _build_launch(stages)
        in_maps = []
        for c in range(NCORES):
            m = {}
            for name in kb.inputs:
                if name == "x_in":
                    m[name] = xs[c]
                elif percore_extra and name in percore_extra:
                    m[name] = percore_extra[name][c]
                elif name in w:
                    m[name] = w[name]
                else:
                    raise KeyError(f"missing input {name}")
            in_maps.append(m)
        res = run_bass_kernel_spmd(nc, in_maps, core_ids=list(range(NCORES)))
        xs = [np.asarray(res.results[c]["x_out"]) for c in range(NCORES)]

    out = np.empty((B, T, HID), np.float32)
    for c in range(NCORES):
        b, q = c // 4, c % 4
        out[b, q * TL:(q + 1) * TL, :] = xs[c].T
    return out


def kernel(**inputs):
    inputs = {k: np.asarray(v) for k, v in inputs.items()}
    x = inputs["L0_attn_hidden_states"]
    w = _prep_weights(inputs)
    pe = _percore_weights(inputs)
    return run_chain(x, w, LAUNCH_GROUPS, pe)


def _percore_weights(inputs):
    pe = {}
    if "L0_attn_Wq_exp" in inputs:
        def xpack(wexp, c):
            outs = []
            for h in range(2):
                gh = 2 * (c % 4) + h
                A = wexp[gh].reshape(2, 128, 6, 2, 128)
                outs.append(A.transpose(1, 2, 3, 0, 4).reshape(128, 3072))
            return np.ascontiguousarray(np.concatenate(outs, axis=1))
        pe["l0a_qexp"] = [xpack(inputs["L0_attn_Wq_exp"], c) for c in range(8)]
        pe["l0a_kexp"] = [xpack(inputs["L0_attn_Wk_exp"], c) for c in range(8)]
    return pe
